# version 03
# baseline (speedup 1.0000x reference)
import numpy as np
import ml_dtypes
from contextlib import ExitStack
import concourse.bass as bass
import concourse.mybir as mybir
from concourse.bass_utils import run_bass_kernel_spmd

F32 = mybir.dt.float32
BF16 = mybir.dt.bfloat16
I8 = mybir.dt.int8
AF = mybir.ActivationFunctionType
ALU = mybir.AluOpType
AX = mybir.AxisListType

S, D = 2048, 1024
NT = S // 128
KC = D // 128
C_RND = 12582912.0
NEG_MASK = -1.0e30
LN_EPS = 1e-5
Q_EPS = 1e-6

ENGINES = ("sync", "gpsimd", "tensor", "vector", "scalar")
SEMS = ("dx0", "dx1", "dr0", "dr1", "do0", "do1",
        "dwi", "dwq", "dwk", "dwv", "dwt", "dwm", "dws", "dwb",
        "ts", "vs", "ss", "gs", "esv")


class Sch:

    def __init__(self):
        self.items = {e: [] for e in ENGINES}
        self.cnt = {s: 0 for s in SEMS}
        self.guaranteed = {e: {s: 0 for s in SEMS} for e in ENGINES}

    def wait(self, eng, sem, val):
        if val is None or val <= 0:
            return
        if val > self.guaranteed[eng][sem]:
            self.items[eng].append(("wait", sem, val))
            self.guaranteed[eng][sem] = val

    def op(self, eng, fn, inc=None, inc_by=1, waits=()):
        for (s, v) in waits:
            self.wait(eng, s, v)
        self.items[eng].append(("op", fn, inc, inc_by))
        if inc:
            self.cnt[inc] += inc_by
            return self.cnt[inc]
        return None


def build_nc():
    nc = bass.Bass("TRN2", target_bir_lowering=False)

    x_h = nc.dram_tensor("x", [S, D], F32, kind="ExternalInput").ap()
    wq_h = nc.dram_tensor("wq", [D, D], F32, kind="ExternalInput").ap()
    wk_h = nc.dram_tensor("wk", [D, D], F32, kind="ExternalInput").ap()
    wv_h = nc.dram_tensor("wv", [D, D], F32, kind="ExternalInput").ap()
    wqt_h = nc.dram_tensor("wqt", [D, D], I8, kind="ExternalInput").ap()
    wsb_h = nc.dram_tensor("wsb", [128, D], F32, kind="ExternalInput").ap()
    bb_h = nc.dram_tensor("bb", [128, D], F32, kind="ExternalInput").ap()
    mask_h = nc.dram_tensor("mask", [128, 128], F32, kind="ExternalInput").ap()
    ident_h = nc.dram_tensor("ident", [128, 128], BF16, kind="ExternalInput").ap()
    y_h = nc.dram_tensor("y", [S, D], F32, kind="ExternalOutput").ap()

    off = [(nc.sbuf_base + 31) & ~31]

    def at(name, shape, dtype):
        nbytes = int(np.prod(shape[1:])) * mybir.dt.size(dtype)
        nbytes = (nbytes + 31) & ~31
        h = nc.alloc_sbuf_tensor_at(name, shape, dtype, offset=off[0])
        off[0] += nbytes
        return h

    ident = at("ident_sb", [128, 128], BF16)
    mask = at("mask_sb", [128, 128], F32)
    st1 = [at(f"st1_{i}", [128, 16], F32) for i in range(2)]
    st3 = [at(f"st3_{i}", [128, 16], F32) for i in range(4)]
    QT = at("QT", [128, KC * S], BF16)
    KT = at("KT", [128, KC * S], BF16)
    Vt = at("Vt", [128, NT * D], BF16)
    wqt = at("wqt_sb", [128, KC * D], BF16)

    R = off[0]
    wqs = at("wq_sb", [128, KC * D], BF16)
    wks = at("wk_sb", [128, KC * D], BF16)
    wvs = at("wv_sb", [128, KC * D], BF16)
    xnT = at("xnT", [128, KC * S], BF16)
    xbuf = [at(f"xbuf{i}", [128, D], F32) for i in range(2)]
    xnbuf = [at(f"xnbuf{i}", [128, D], BF16) for i in range(2)]
    p12_end = off[0]
    assert p12_end <= nc.sbuf_top, f"p12 SBUF overflow: {p12_end} > {nc.sbuf_top}"
    off[0] = R
    wsb = at("wsb_sb", [128, D], F32)
    bb = at("bb_sb", [128, D], F32)
    A_sb = at("A_sb", [128, S], BF16)
    AT_sb = at("AT_sb", [128, S], BF16)
    xr = [at(f"xr{i}", [128, D], F32) for i in range(2)]
    x1 = [at(f"x1_{i}", [128, D], F32) for i in range(2)]
    x1b = [at(f"x1b{i}", [128, D], F32) for i in range(2)]
    xn1 = at("xn1", [128, D], F32)
    xqf = [at(f"xqf{i}", [128, D], F32) for i in range(2)]
    xq = [at(f"xq{i}", [128, D], BF16) for i in range(2)]
    xqT = [at(f"xqT{i}", [128, D], BF16) for i in range(2)]
    t2 = at("t2", [128, D], F32)
    outf = [at(f"outf{i}", [128, D], F32) for i in range(2)]
    assert off[0] <= p12_end
    nc.sbuf_base = p12_end

    psBF = nc.alloc_psum_tensor("psBF", [128, 2048], BF16)
    psF1 = nc.alloc_psum_tensor("psF1", [128, 1024], F32)
    psF2 = nc.alloc_psum_tensor("psF2", [128, 1024], F32)
    psF3 = nc.alloc_psum_tensor("psF3", [128, 1024], F32)

    def psT(par):
        return psBF[:, par * 1024:(par + 1) * 1024]

    def psQ(b):
        t_ = (psF1, psF2, psF3)[b // 2]
        return t_[:, (b % 2) * 512:(b % 2 + 1) * 512]

    def score_ap(b, n):
        t_ = psF1 if b < 2 else psF2
        return t_[:, (b % 2) * 512:(b % 2) * 512 + n]

    psAT = [psBF[:, 0:1024], psBF[:, 1024:2048]]
    psXQ = psBF[:, 1024:2048]
    psMM = psF3

    sch = Sch()

    with ExitStack() as ctx:
        e = ctx.enter_context
        sems = {s: e(nc.semaphore(s)) for s in SEMS}

        def dma(out_ap, in_ap):
            return lambda g: g.dma_start(out_ap, in_ap)

        m_eps = sch.op("vector", (lambda v: v.memset(st1[0][:, 15:16], LN_EPS)),
                       inc="vs")
        m_ident = sch.op("gpsimd", dma(ident[:], ident_h), inc="dwi", inc_by=16)
        m_wq = sch.op("gpsimd", dma(
            wqs[:].rearrange("p (c d) -> p c d", d=D),
            wq_h.rearrange("(c p) d -> p c d", p=128)), inc="dwq", inc_by=16)
        m_wk = sch.op("gpsimd", dma(
            wks[:].rearrange("p (c d) -> p c d", d=D),
            wk_h.rearrange("(c p) d -> p c d", p=128)), inc="dwk", inc_by=16)
        m_wv = sch.op("gpsimd", dma(
            wvs[:].rearrange("p (c d) -> p c d", d=D),
            wv_h.rearrange("(c p) d -> p c d", p=128)), inc="dwv", inc_by=16)
        m_wqt = sch.op("gpsimd", dma(
            wqt[:].rearrange("p (c d) -> p c d", d=D),
            wqt_h.rearrange("(c p) d -> p c d", p=128)), inc="dwt", inc_by=16)
        m_mask = sch.op("gpsimd", dma(mask[:], mask_h), inc="dwm", inc_by=16)

        m_dx = {}
        m_red = {}
        m_v1 = {}
        m_xn = {}
        m_trT = {}
        m_evT = {}
        NQB = 6
        g_ctr = [0]
        n_groups_total = 96
        pending_evacs = []

        def emit_qkv_group(kind, c_or_tt, sl, tok_ms):
            g = g_ctr[0]
            g_ctr[0] += 1
            b = g % NQB
            waits = [("vs", tok_ms),
                     ({"q": "dwq", "k": "dwk", "v": "dwv"}[kind],
                      {"q": m_wq, "k": m_wk, "v": m_wv}[kind])]
            if g >= NQB:
                waits.append(("esv", g - NQB + 1))
            m_g = None
            for kc in range(KC):
                if kind in ("q", "k"):
                    wt = wqs if kind == "q" else wks
                    lhsT = wt[:, kc * D + c_or_tt * 128: kc * D + (c_or_tt + 1) * 128]
                    rhs = xnT[:, kc * S + sl * 512: kc * S + (sl + 1) * 512]
                else:
                    lhsT = xnT[:, kc * S + c_or_tt * 128: kc * S + (c_or_tt + 1) * 128]
                    rhs = wvs[:, kc * D + sl * 512: kc * D + (sl + 1) * 512]
                st = (kc == 0)
                sp = (kc == KC - 1)
                fn = (lambda lhsT=lhsT, rhs=rhs, st=st, sp=sp, b=b:
                      (lambda t: t.matmul(psQ(b), lhsT, rhs, start=st, stop=sp)))()
                m_g = sch.op("tensor", fn, inc=("ts" if sp else None),
                             waits=(waits if kc == 0 else ()))
            if kind == "q":
                dst = QT[:, c_or_tt * S + sl * 512: c_or_tt * S + (sl + 1) * 512]
            elif kind == "k":
                dst = KT[:, c_or_tt * S + sl * 512: c_or_tt * S + (sl + 1) * 512]
            else:
                dst = Vt[:, c_or_tt * D + sl * 512: c_or_tt * D + (sl + 1) * 512]
            pending_evacs.append((dst, b, m_g))

        def flush_evacs(n):
            k = 0
            while pending_evacs and k < n:
                dst, b, m_g = pending_evacs.pop(0)
                fn = (lambda dst=dst, b=b: (lambda v: v.tensor_copy(dst, psQ(b))))()
                sch.op("vector", fn, inc="esv", waits=[("ts", m_g)])
                k += 1

        for t in range(NT + 2):
            h = t - 2
            if 0 <= h < NT:
                par = h % 2
                stt = st1[par]
                sch.op("scalar", (lambda stt=stt: (lambda s: s.activation(
                    stt[:, 4:5], stt[:, 3:4], AF.Ln, bias=st1[0][:, 15:16])))(),
                    inc="ss", waits=[("vs", m_v1[h]), ("vs", m_eps)])
                m_lnm = sch.cnt["ss"]
                sch.op("scalar", (lambda stt=stt: (lambda s: s.activation(
                    stt[:, 5:6], stt[:, 4:5], AF.Exp, scale=-0.5)))(),
                    inc="ss", waits=[("ss", m_lnm)])
                m_rstd = sch.cnt["ss"]
                sch.op("scalar", (lambda stt=stt: (lambda s: s.activation(
                    stt[:, 6:7], stt[:, 1:2], AF.Identity, scale=stt[:, 5:6])))(),
                    inc="ss", waits=[("ss", m_rstd)])
                m_b2 = sch.cnt["ss"]
                m_xn[h] = sch.op("scalar", (lambda par=par, stt=stt: (lambda s: s.activation(
                    xnbuf[par][:], xbuf[par][:], AF.Identity,
                    scale=stt[:, 5:6], bias=stt[:, 6:7])))(),
                    inc="ss", waits=[("ss", m_b2)])
                w = [("ss", m_xn[h]), ("dwi", m_ident)]
                if h >= 2:
                    w.append(("vs", m_evT[h - 2]))
                for c in range(KC):
                    fn = (lambda par=par, c=c: (lambda te: te.transpose(
                        psT(par)[:, c * 128:(c + 1) * 128],
                        xnbuf[par][:, c * 128:(c + 1) * 128], ident[:])))()
                    m_trT[h] = sch.op("tensor", fn,
                                      inc=("ts" if c == KC - 1 else None),
                                      waits=(w if c == 0 else ()))
            if 0 <= h < NT and h % 4 == 0 and h >= 4:
                flush_evacs(10 ** 9)
            if t < NT:
                par = t % 2
                stt = st1[par]
                w = []
                if t >= 2:
                    w = [("ss", m_xn[t - 2]), ("vs", m_red[t - 2])]
                m_dx[t] = sch.op("sync",
                                 dma(xbuf[par][:], x_h[t * 128:(t + 1) * 128, :]),
                                 inc=f"dx{par}", inc_by=16, waits=w)
                m_red[t] = sch.op(
                    "vector", (lambda par=par, stt=stt: (lambda v: v.reduce_sum(
                        stt[:, 0:1], xbuf[par][:], axis=AX.X)))(),
                    inc="vs", waits=[(f"dx{par}", m_dx[t])])
                sch.op("vector", (lambda stt=stt: (lambda v: v.tensor_scalar(
                    stt[:, 1:2], stt[:, 0:1], -1.0 / D, None, ALU.mult)))(),
                    inc="vs", waits=[("vs", m_red[t])])
                m_negm = sch.cnt["vs"]
                sch.op("vector", (lambda stt=stt: (lambda v: v.tensor_mul(
                    stt[:, 7:8], stt[:, 1:2], stt[:, 1:2])))(),
                    inc="vs", waits=[("vs", m_negm)])
                m_mm = sch.cnt["vs"]
                m_sq = sch.op("scalar", (lambda par=par, stt=stt: (lambda s: s.activation(
                    xnbuf[par][:], xbuf[par][:], AF.Square,
                    accum_out=stt[:, 2:3])))(),
                    inc="ss", waits=[(f"dx{par}", m_dx[t]),
                                     ("ts", m_trT.get(t - 2, 0))])
                m_v1[t] = sch.op("vector", (lambda stt=stt: (lambda v: v.scalar_tensor_tensor(
                    stt[:, 3:4], stt[:, 2:3], 1.0 / D, stt[:, 7:8],
                    ALU.mult, ALU.subtract)))(),
                    inc="vs", waits=[("ss", m_sq), ("vs", m_mm)])
            if 0 <= h < NT:
                par = h % 2
                m_evT[h] = sch.op("vector", (lambda par=par, h=h: (lambda v: v.tensor_copy(
                    xnT[:].rearrange("p (c s) -> p c s", s=S)[:, :, h * 128:(h + 1) * 128],
                    psBF[:].rearrange("p (q c u) -> p q c u", q=2, u=128)[:, par])))(),
                    inc="vs", waits=[("ts", m_trT[h])])
                if h % 4 == 3:
                    s4 = h // 4
                    for c in range(KC):
                        emit_qkv_group("q", c, s4, m_evT[h])
                    for c in range(KC):
                        emit_qkv_group("k", c, s4, m_evT[h])
                    for tt in range(s4 * 4, s4 * 4 + 4):
                        for s2 in range(2):
                            emit_qkv_group("v", tt, s2, m_evT[tt])
        flush_evacs(10 ** 9)
        assert g_ctr[0] == n_groups_total

        m_ph2_end = sch.cnt["ts"]

        m_wsb = sch.op("gpsimd", dma(wsb[:], wsb_h), inc="dws", inc_by=16,
                       waits=[("ts", m_ph2_end)])
        m_bb = sch.op("gpsimd", dma(bb[:], bb_h), inc="dwb", inc_by=16)

        m_exp = {}
        m_trA = {}
        m_evATa = {}
        m_evATb = {}
        m_av = {}
        m_x1 = {}
        m_x1b = {}
        m_sq2 = {}
        m_xn1 = {}
        m_xqf = {}
        m_rnd = {}
        m_trX = {}
        m_evXQ = {}
        m_ffn = {}
        m_stt2 = {}
        m_out = {}
        m_store = {}
        m_xr = {}
        m_amax = {}
        m_inv = {}
        m_b3m = {}
        m_negm2 = {}
        m_rv2 = {}
        m_b22 = {}
        m_sc = {}
        m_mm2h = {}

        for it in range(NT + 2):
            i = it
            h = it - 1
            j = it - 2
            par = i % 2
            hpar = h % 2
            jpar = j % 2
            stt = st3[i % 4]
            hst = st3[h % 4]
            jst = st3[j % 4]
            nch = i + 1
            nb = nch * 128

            if i < NT:
                w = [("ts", m_ph2_end)]
                if i >= 2:
                    w.append(("vs", m_x1[i - 2]))
                m_xr[i] = sch.op("sync",
                                 dma(xr[par][:], x_h[i * 128:(i + 1) * 128, :]),
                                 inc=f"dr{par}", inc_by=16, waits=w)

            if i < NT:
                w = [("esv", n_groups_total), ("vs", m_evT[NT - 1])]
                if i >= 1:
                    w.append(("ss", m_exp[i - 1]))
                nbanks = (nb + 511) // 512
                for kc in range(KC):
                    lhsT = QT[:, kc * S + i * 128: kc * S + (i + 1) * 128]
                    for b in range(nbanks):
                        n = min(512, nb - b * 512)
                        rhs = KT[:, kc * S + b * 512: kc * S + b * 512 + n]
                        st = (kc == 0)
                        sp = (kc == KC - 1)
                        last = sp and (b == nbanks - 1)
                        fn = (lambda lhsT=lhsT, rhs=rhs, st=st, sp=sp, b=b, n=n:
                              (lambda te: te.matmul(score_ap(b, n), lhsT, rhs,
                                                    start=st, stop=sp)))()
                        m_sc[i] = sch.op("tensor", fn, inc=("ts" if last else None),
                                         waits=(w if (b == 0 and kc == 0) else ()))

                if h >= 0:
                    sch.op("vector", (lambda hpar=hpar, hst=hst: (lambda v: v.reduce_sum(
                        hst[:, 4:5], x1[hpar][:], axis=AX.X)))(),
                        inc="vs", waits=[("vs", m_x1[h])])
                    m_red2 = sch.cnt["vs"]
                    sch.op("vector", (lambda hst=hst: (lambda v: v.tensor_scalar(
                        hst[:, 5:6], hst[:, 4:5], -1.0 / D, None, ALU.mult)))(),
                        inc="vs", waits=[("vs", m_red2)])
                    m_negm2[h] = sch.cnt["vs"]
                    sch.op("vector", (lambda hst=hst: (lambda v: v.tensor_mul(
                        hst[:, 7:8], hst[:, 5:6], hst[:, 5:6])))(),
                        inc="vs", waits=[("vs", m_negm2[h])])
                    m_mm2h[h] = sch.cnt["vs"]
                diag = (psF1[:, i * 128:(i + 1) * 128] if i < 8
                        else psF2[:, (i - 8) * 128:(i - 7) * 128])
                w = [("ts", m_sc[i]), ("dwm", m_mask)]
                if i >= 4:
                    w += [("vs", m_stt2.get(i - 4, 0)), ("ss", m_xqf.get(i - 4, 0))]
                sch.op("vector", (lambda diag=diag: (lambda v: v.tensor_add(
                    diag, diag, mask[:])))(), inc="vs", waits=w)
                m_maskd = sch.cnt["vs"]
                n1 = min(nb, 1024)
                sch.op("vector", (lambda n1=n1, stt=stt: (lambda v: v.reduce_max(
                    stt[:, 0:1], psF1[:, 0:n1], axis=AX.X, negate=True)))(),
                    inc="vs", waits=[("vs", m_maskd)])
                if nb > 1024:
                    sch.op("vector", (lambda nb=nb, stt=stt: (lambda v: v.reduce_max(
                        stt[:, 15:16], psF2[:, 0:nb - 1024], axis=AX.X,
                        negate=True)))(), inc="vs")
                    sch.op("vector", (lambda stt=stt: (lambda v: v.tensor_tensor(
                        stt[:, 0:1], stt[:, 0:1], stt[:, 15:16], ALU.min)))(),
                        inc="vs", waits=[("vs", sch.cnt["vs"])])
                m_nmax = sch.cnt["vs"]
                sch.op("vector", (lambda stt=stt: (lambda v: v.tensor_scalar(
                    stt[:, 1:2], stt[:, 0:1], 1.0 / 32.0, None, ALU.mult)))(),
                    inc="vs", waits=[("vs", m_nmax)])
                m_b3m[i] = sch.cnt["vs"]
            elif h >= 0 and h < NT:
                sch.op("vector", (lambda hpar=hpar, hst=hst: (lambda v: v.reduce_sum(
                    hst[:, 4:5], x1[hpar][:], axis=AX.X)))(),
                    inc="vs", waits=[("vs", m_x1[h])])
                m_red2 = sch.cnt["vs"]
                sch.op("vector", (lambda hst=hst: (lambda v: v.tensor_scalar(
                    hst[:, 5:6], hst[:, 4:5], -1.0 / D, None, ALU.mult)))(),
                    inc="vs", waits=[("vs", m_red2)])
                m_negm2[h] = sch.cnt["vs"]
                sch.op("vector", (lambda hst=hst: (lambda v: v.tensor_mul(
                    hst[:, 7:8], hst[:, 5:6], hst[:, 5:6])))(),
                    inc="vs", waits=[("vs", m_negm2[h])])
                m_mm2h[h] = sch.cnt["vs"]

            if j >= 0:
                w = [("vs", m_rnd[j]),
                     ("ss", m_evATb.get(i - 1, m_evXQ.get(j - 1, 0)))]
                for c in range(KC):
                    fn = (lambda c=c, jpar=jpar: (lambda te: te.transpose(
                        psXQ[:, c * 128:(c + 1) * 128],
                        xq[jpar][:, c * 128:(c + 1) * 128], ident[:])))()
                    m_trX[j] = sch.op("tensor", fn,
                                      inc=("ts" if c == KC - 1 else None),
                                      waits=(w if c == 0 else ()))

            if 0 <= h < NT:
                m_sq2[h] = sch.op("scalar", (lambda hpar=hpar, hst=hst: (lambda s: s.activation(
                    A_sb[:, 0:1024], x1[hpar][:], AF.Square,
                    accum_out=hst[:, 6:7])))(),
                    inc="ss", waits=[("vs", m_x1[h])])
                sch.op("vector", (lambda hst=hst: (lambda v: v.scalar_tensor_tensor(
                    hst[:, 8:9], hst[:, 6:7], 1.0 / D, hst[:, 7:8],
                    ALU.mult, ALU.subtract)))(),
                    inc="vs", waits=[("ss", m_sq2[h]), ("vs", m_mm2h[h])])
                m_rv2[h] = sch.cnt["vs"]
                sch.op("scalar", (lambda hst=hst: (lambda s: s.activation(
                    hst[:, 9:10], hst[:, 8:9], AF.Ln, bias=st1[0][:, 15:16])))(),
                    inc="ss", waits=[("vs", m_rv2[h])])
                m_ln2 = sch.cnt["ss"]
                sch.op("scalar", (lambda hst=hst: (lambda s: s.activation(
                    hst[:, 10:11], hst[:, 9:10], AF.Exp, scale=-0.5)))(),
                    inc="ss", waits=[("ss", m_ln2)])
                m_rstd2 = sch.cnt["ss"]
                sch.op("scalar", (lambda hst=hst: (lambda s: s.activation(
                    hst[:, 11:12], hst[:, 5:6], AF.Identity,
                    scale=hst[:, 10:11])))(),
                    inc="ss", waits=[("ss", m_rstd2)])
                m_b22[h] = sch.cnt["ss"]
                m_xn1[h] = sch.op("scalar", (lambda hpar=hpar, hst=hst: (lambda s: s.activation(
                    xn1[:], x1[hpar][:], AF.Identity,
                    scale=hst[:, 10:11], bias=hst[:, 11:12])))(),
                    inc="ss", waits=[("ss", m_b22[h]), ("vs", m_amax.get(h - 1, 0))])
                sch.op("vector", (lambda hst=hst: (lambda v: v.reduce_max(
                    hst[:, 12:13], xn1[:], axis=AX.X, apply_absolute_value=True)))(),
                    inc="vs", waits=[("ss", m_xn1[h])])
                m_amax[h] = sch.cnt["vs"]
                sch.op("vector", (lambda hst=hst: (lambda v: v.tensor_scalar(
                    hst[:, 13:14], hst[:, 12:13], Q_EPS, 1.0 / 127.0,
                    ALU.max, ALU.mult)))(),
                    inc="vs", waits=[("vs", m_amax[h])])
                m_xsc = sch.cnt["vs"]
                sch.op("vector", (lambda hst=hst: (lambda v: v.reciprocal(
                    hst[:, 14:15], hst[:, 13:14])))(),
                    inc="vs", waits=[("vs", m_xsc)])
                m_inv[h] = sch.cnt["vs"]
                m_xqf[h] = sch.op("scalar", (lambda hpar=hpar, hst=hst: (lambda s: s.activation(
                    xqf[hpar][:], xn1[:], AF.Identity, scale=hst[:, 14:15])))(),
                    inc="ss", waits=[("vs", m_inv[h]), ("vs", m_rnd.get(h - 2, 0))])
                m_rnd[h] = sch.op("vector", (lambda hpar=hpar: (lambda v: v.tensor_scalar(
                    xq[hpar][:], xqf[hpar][:], C_RND, C_RND,
                    ALU.add, ALU.subtract)))(),
                    inc="vs", waits=[("ss", m_xqf[h]), ("ts", m_trX.get(h - 2, 0))])

            if j >= 0:
                m_evXQ[j] = sch.op("scalar", (lambda jpar=jpar: (lambda s: s.copy(
                    xqT[jpar][:], psXQ[:])))(),
                    inc="ss", waits=[("ts", m_trX[j])])
                w = [("ss", m_evXQ[j]), ("dwt", m_wqt),
                     ("vs", m_x1.get(j + 1, 0)), ("vs", m_stt2.get(j - 1, 0))]
                first = True
                for kc in range(KC):
                    for sl in range(2):
                        st = (kc == 0)
                        sp = (kc == KC - 1)
                        fn = (lambda kc=kc, sl=sl, st=st, sp=sp, jpar=jpar:
                              (lambda te: te.matmul(
                                  psMM[:, sl * 512:(sl + 1) * 512],
                                  xqT[jpar][:, kc * 128:(kc + 1) * 128],
                                  wqt[:, kc * D + sl * 512: kc * D + (sl + 1) * 512],
                                  start=st, stop=sp)))()
                        m_ffn[j] = sch.op("tensor", fn,
                                          inc=("ts" if (sp and sl == 1) else None),
                                          waits=(w if first else ()))
                        first = False

                m_stt2[j] = sch.op("vector", (lambda jst=jst: (lambda v: v.scalar_tensor_tensor(
                    t2[:], psMM[:], jst[:, 13:14], wsb[:], ALU.mult, ALU.mult)))(),
                    inc="vs", waits=[("ts", m_ffn[j]), ("dws", m_wsb),
                                     ("gs", m_out.get(j - 1, 0))])

                m_out[j] = sch.op("gpsimd", (lambda jpar=jpar: (lambda g: g.tensor_add(
                    outf[jpar][:], t2[:], x1b[jpar][:])))(),
                    inc="gs", waits=[("vs", m_stt2[j]), ("gs", m_x1b[j]),
                                     (f"do{jpar}", m_store.get(j - 2, 0))])

                m_store[j] = sch.op("sync",
                                    dma(y_h[j * 128:(j + 1) * 128, :], outf[jpar][:]),
                                    inc=f"do{jpar}", inc_by=16,
                                    waits=[("gs", m_out[j])])

            if i < NT:
                n1 = min(nb, 1024)
                w = [("vs", m_b3m[i])]
                if i >= 1:
                    w.append(("ts", m_trA[i - 1]))
                m_exp[i] = sch.op("scalar", (lambda n1=n1, stt=stt: (lambda s: s.activation(
                    A_sb[:, 0:n1], psF1[:, 0:n1], AF.Exp, scale=1.0 / 32.0,
                    bias=stt[:, 1:2], accum_out=stt[:, 2:3])))(),
                    inc="ss", waits=w)
                if nb > 1024:
                    m_exp[i] = sch.op("scalar", (lambda nb=nb, stt=stt: (lambda s: s.activation(
                        A_sb[:, 1024:nb], psF2[:, 0:nb - 1024], AF.Exp,
                        scale=1.0 / 32.0, bias=stt[:, 1:2],
                        accum_out=stt[:, 14:15])))(), inc="ss")

                ha = (nch + 1) // 2
                rounds = [(0, ha, 0), (ha, nch, 1)]
                for (lo, hi, bk) in rounds:
                    if lo >= hi:
                        continue
                    w = [("ss", m_exp[i])]
                    if bk == 0:
                        w.append(("ss", m_evATa.get(i - 1, 0)))
                    else:
                        w.append(("ss", m_evXQ.get(j, m_evATb.get(i - 1, 0))))
                        w.append(("ss", m_evATb.get(i - 1, 0)))
                    for jj in range(lo, hi):
                        fn = (lambda jj=jj, lo=lo, bk=bk: (lambda te: te.transpose(
                            psAT[bk][:, (jj - lo) * 128:(jj - lo + 1) * 128],
                            A_sb[:, jj * 128:(jj + 1) * 128], ident[:])))()
                        m_trA[i] = sch.op("tensor", fn,
                                          inc=("ts" if jj == hi - 1 else None),
                                          waits=(w if jj == lo else ()))
                    nb_r = (hi - lo) * 128
                    mev = sch.op("scalar", (lambda lo=lo, nb_r=nb_r, bk=bk: (lambda s: s.copy(
                        AT_sb[:, lo * 128: lo * 128 + nb_r], psAT[bk][:, 0:nb_r])))(),
                        inc="ss", waits=[("ts", m_trA[i])])
                    if bk == 0:
                        m_evATa[i] = mev
                    else:
                        m_evATb[i] = mev
                if nch == 1:
                    m_evATb[i] = m_evATa[i]

                w = [("ss", m_evATb[i]), ("ss", m_evATa[i]),
                     ("vs", m_stt2.get(i - 2, 0))]
                first = True
                for jj in range(nch):
                    for sl in range(2):
                        st = (jj == 0)
                        sp = (jj == nch - 1)
                        fn = (lambda jj=jj, sl=sl, st=st, sp=sp: (lambda te: te.matmul(
                            psMM[:, sl * 512:(sl + 1) * 512],
                            AT_sb[:, jj * 128:(jj + 1) * 128],
                            Vt[:, jj * D + sl * 512: jj * D + (sl + 1) * 512],
                            start=st, stop=sp)))()
                        m_av[i] = sch.op("tensor", fn,
                                         inc=("ts" if (sp and sl == 1) else None),
                                         waits=(w if first else ()))
                        first = False

                if nb > 1024:
                    sch.op("vector", (lambda stt=stt: (lambda v: v.tensor_add(
                        stt[:, 2:3], stt[:, 2:3], stt[:, 14:15])))(),
                        inc="vs", waits=[("ss", m_exp[i])])
                    lw = [("vs", sch.cnt["vs"])]
                else:
                    lw = [("ss", m_exp[i])]
                sch.op("vector", (lambda stt=stt: (lambda v: v.reciprocal(
                    stt[:, 3:4], stt[:, 2:3])))(),
                    inc="vs", waits=lw)
                m_linv = sch.cnt["vs"]
                m_x1[i] = sch.op("vector", (lambda par=par, stt=stt: (lambda v: v.scalar_tensor_tensor(
                    x1[par][:], psMM[:], stt[:, 3:4], xr[par][:],
                    ALU.mult, ALU.add)))(),
                    inc="vs", waits=[("ts", m_av[i]), (f"dr{par}", m_xr[i]),
                                     ("vs", m_linv),
                                     ("gs", m_x1b.get(i - 2, 0)),
                                     ("ss", m_xn1.get(i - 2, 0))])

                m_x1b[i] = sch.op("gpsimd", (lambda par=par: (lambda g: g.tensor_add(
                    x1b[par][:], x1[par][:], bb[:])))(),
                    inc="gs", waits=[("vs", m_x1[i]), ("dwb", m_bb),
                                     ("gs", m_out.get(i - 2, 0))])

        sch.wait("sync", "do0", m_store[NT - 2])
        sch.wait("sync", "do1", m_store[NT - 1])

        block = e(nc.Block())

        def replay(name):
            def f(eng):
                for itm in sch.items[name]:
                    if itm[0] == "wait":
                        eng.wait_ge(sems[itm[1]], itm[2])
                    else:
                        _, fn, inc, by = itm
                        instr = fn(eng)
                        if inc:
                            instr.then_inc(sems[inc], by)
            return f

        block.sync(replay("sync"))
        block.gpsimd(replay("gpsimd"))
        block.tensor(replay("tensor"))
        block.vector(replay("vector"))
        block.scalar(replay("scalar"))

    return nc


_NC_CACHE = None


def _get_nc():
    global _NC_CACHE
    if _NC_CACHE is None:
        _NC_CACHE = build_nc()
    return _NC_CACHE


def kernel(L0_attn_x, L0_attn_Wk, L0_attn_Wq, L0_attn_Wv,
           L0_ffn_weight_q, L0_ffn_weight_scale, L0_ffn_bias):
    x = np.ascontiguousarray(np.asarray(L0_attn_x, dtype=np.float32))
    wk = np.ascontiguousarray(np.asarray(L0_attn_Wk, dtype=np.float32))
    wq = np.ascontiguousarray(np.asarray(L0_attn_Wq, dtype=np.float32))
    wv = np.ascontiguousarray(np.asarray(L0_attn_Wv, dtype=np.float32))
    wqt = np.ascontiguousarray(np.asarray(L0_ffn_weight_q).T)
    wsb = np.ascontiguousarray(np.broadcast_to(
        np.asarray(L0_ffn_weight_scale, dtype=np.float32)[None, :], (128, D)))
    bb = np.ascontiguousarray(np.broadcast_to(
        np.asarray(L0_ffn_bias, dtype=np.float32)[None, :], (128, D)))
    r = np.arange(128)
    mask = np.where(r[:, None] >= r[None, :], 0.0, NEG_MASK).astype(np.float32)
    ident = np.eye(128).astype(ml_dtypes.bfloat16)

    nc = _get_nc()
    in_maps = [dict(x=x[b], wq=wq, wk=wk, wv=wv, wqt=wqt,
                    wsb=wsb, bb=bb, mask=mask, ident=ident) for b in range(8)]
    res = run_bass_kernel_spmd(nc, in_maps, core_ids=list(range(8)))
    out = np.stack([np.asarray(res.results[b]["y"]) for b in range(8)], axis=0)
    return out.astype(np.float32)


# version 04
# speedup vs baseline: 1.2216x; 1.2216x over previous
import numpy as np
import ml_dtypes
from contextlib import ExitStack
import concourse.bass as bass
import concourse.mybir as mybir
from concourse.bass_utils import run_bass_kernel_spmd

F32 = mybir.dt.float32
BF16 = mybir.dt.bfloat16
I8 = mybir.dt.int8
AF = mybir.ActivationFunctionType
ALU = mybir.AluOpType
AX = mybir.AxisListType

S, D = 2048, 1024
NT = S // 128
KC = D // 128
C_RND = 12582912.0
NEG_MASK = -1.0e30
LN_EPS = 1e-5
Q_EPS = 1e-6

ENGINES = ("sync", "gpsimd", "tensor", "vector", "scalar")
SEMS = ("dx0", "dx1", "dr0", "dr1", "do0", "do1",
        "dwi", "dwq", "dwk", "dwv", "dwt", "dwm", "dws", "dwb",
        "ts", "vs", "ss", "gs", "esv")


class Sch:

    def __init__(self):
        self.items = {e: [] for e in ENGINES}
        self.cnt = {s: 0 for s in SEMS}
        self.guaranteed = {e: {s: 0 for s in SEMS} for e in ENGINES}

    def wait(self, eng, sem, val):
        if val is None or val <= 0:
            return
        if val > self.guaranteed[eng][sem]:
            self.items[eng].append(("wait", sem, val))
            self.guaranteed[eng][sem] = val

    def op(self, eng, fn, inc=None, inc_by=1, waits=()):
        for (s, v) in waits:
            self.wait(eng, s, v)
        self.items[eng].append(("op", fn, inc, inc_by))
        if inc:
            self.cnt[inc] += inc_by
            return self.cnt[inc]
        return None


def build_nc():
    nc = bass.Bass("TRN2", target_bir_lowering=False)

    x_h = nc.dram_tensor("x", [S, D], F32, kind="ExternalInput").ap()
    wq_h = nc.dram_tensor("wq", [D, D], F32, kind="ExternalInput").ap()
    wk_h = nc.dram_tensor("wk", [D, D], F32, kind="ExternalInput").ap()
    wv_h = nc.dram_tensor("wv", [D, D], F32, kind="ExternalInput").ap()
    wqt_h = nc.dram_tensor("wqt", [D, D], I8, kind="ExternalInput").ap()
    wsb_h = nc.dram_tensor("wsb", [128, D], F32, kind="ExternalInput").ap()
    bb_h = nc.dram_tensor("bb", [128, D], F32, kind="ExternalInput").ap()
    mask_h = nc.dram_tensor("mask", [128, 128], F32, kind="ExternalInput").ap()
    ident_h = nc.dram_tensor("ident", [128, 128], BF16, kind="ExternalInput").ap()
    y_h = nc.dram_tensor("y", [S, D], F32, kind="ExternalOutput").ap()

    off = [(nc.sbuf_base + 31) & ~31]

    def at(name, shape, dtype):
        nbytes = int(np.prod(shape[1:])) * mybir.dt.size(dtype)
        nbytes = (nbytes + 31) & ~31
        h = nc.alloc_sbuf_tensor_at(name, shape, dtype, offset=off[0])
        off[0] += nbytes
        return h

    junk = at("junk_sb", [128, D], BF16)
    ident = at("ident_sb", [128, 128], BF16)
    mask = at("mask_sb", [128, 128], F32)
    st1 = [at(f"st1_{i}", [128, 16], F32) for i in range(2)]
    st3 = [at(f"st3_{i}", [128, 16], F32) for i in range(4)]
    QT = at("QT", [128, KC * S], BF16)
    KT = at("KT", [128, KC * S], BF16)
    Vt = at("Vt", [128, NT * D], BF16)
    wqt = at("wqt_sb", [128, KC * D], BF16)

    R = off[0]
    wqs = at("wq_sb", [128, KC * D], BF16)
    wks = at("wk_sb", [128, KC * D], BF16)
    wvs = at("wv_sb", [128, KC * D], BF16)
    xnT = at("xnT", [128, KC * S], BF16)
    xbuf = [at(f"xbuf{i}", [128, D], F32) for i in range(2)]
    xnbuf = [at(f"xnbuf{i}", [128, D], BF16) for i in range(2)]
    p12_end = off[0]
    assert p12_end <= nc.sbuf_top, f"p12 SBUF overflow: {p12_end} > {nc.sbuf_top}"
    off[0] = R
    wsb = at("wsb_sb", [128, D], F32)
    bb = at("bb_sb", [128, D], F32)
    A_sb = at("A_sb", [128, S], BF16)
    AT_sb = at("AT_sb", [128, S], BF16)
    xr = [at(f"xr{i}", [128, D], F32) for i in range(2)]
    x1 = [at(f"x1_{i}", [128, D], F32) for i in range(2)]
    x1b = [at(f"x1b{i}", [128, D], F32) for i in range(2)]
    xn1 = at("xn1", [128, D], F32)
    xqf = [at(f"xqf{i}", [128, D], F32) for i in range(2)]
    xq = [at(f"xq{i}", [128, D], BF16) for i in range(2)]
    xqT = [at(f"xqT{i}", [128, D], BF16) for i in range(2)]
    t2 = at("t2", [128, D], F32)
    outf = [at(f"outf{i}", [128, D], F32) for i in range(2)]
    assert off[0] <= p12_end
    nc.sbuf_base = p12_end

    psBF = nc.alloc_psum_tensor("psBF", [128, 2048], BF16)
    psF1 = nc.alloc_psum_tensor("psF1", [128, 1024], F32)
    psF2 = nc.alloc_psum_tensor("psF2", [128, 1024], F32)
    psF3 = nc.alloc_psum_tensor("psF3", [128, 1024], F32)

    def psT(par):
        return psBF[:, par * 1024:(par + 1) * 1024]

    def psQ(b):
        t_ = (psF1, psF2, psF3)[b // 2]
        return t_[:, (b % 2) * 512:(b % 2 + 1) * 512]

    def score_ap(b, n):
        t_ = psF1 if b < 2 else psF2
        return t_[:, (b % 2) * 512:(b % 2) * 512 + n]

    psAT = [psBF[:, 0:1024], psBF[:, 1024:2048]]
    psXQ = psBF[:, 1024:2048]
    psMM = psF3

    sch = Sch()

    with ExitStack() as ctx:
        e = ctx.enter_context
        sems = {s: e(nc.semaphore(s)) for s in SEMS}

        def dma(out_ap, in_ap):
            return lambda g: g.dma_start(out_ap, in_ap)

        m_eps = sch.op("vector", (lambda v: v.memset(st1[0][:, 15:16], LN_EPS)),
                       inc="vs")
        m_ident = sch.op("gpsimd", dma(ident[:], ident_h), inc="dwi", inc_by=16)
        m_wq = sch.op("gpsimd", dma(
            wqs[:].rearrange("p (c d) -> p c d", d=D),
            wq_h.rearrange("(c p) d -> p c d", p=128)), inc="dwq", inc_by=16)
        m_wk = sch.op("gpsimd", dma(
            wks[:].rearrange("p (c d) -> p c d", d=D),
            wk_h.rearrange("(c p) d -> p c d", p=128)), inc="dwk", inc_by=16)
        m_wv = sch.op("gpsimd", dma(
            wvs[:].rearrange("p (c d) -> p c d", d=D),
            wv_h.rearrange("(c p) d -> p c d", p=128)), inc="dwv", inc_by=16)
        m_wqt = sch.op("gpsimd", dma(
            wqt[:].rearrange("p (c d) -> p c d", d=D),
            wqt_h.rearrange("(c p) d -> p c d", p=128)), inc="dwt", inc_by=16)
        m_mask = sch.op("gpsimd", dma(mask[:], mask_h), inc="dwm", inc_by=16)

        m_dx = {}
        m_junkw = [0]
        m_red = {}
        m_v1 = {}
        m_xn = {}
        m_trT = {}
        m_evT = {}
        NQB = 6
        g_ctr = [0]
        n_groups_total = 96
        pending_evacs = []

        def emit_qkv_group(kind, c_or_tt, sl, tok_ms):
            g = g_ctr[0]
            g_ctr[0] += 1
            b = g % NQB
            waits = [("vs", tok_ms),
                     ({"q": "dwq", "k": "dwk", "v": "dwv"}[kind],
                      {"q": m_wq, "k": m_wk, "v": m_wv}[kind])]
            if g >= NQB:
                waits.append(("esv", g - NQB + 1))
            m_g = None
            for kc in range(KC):
                if kind in ("q", "k"):
                    wt = wqs if kind == "q" else wks
                    lhsT = wt[:, kc * D + c_or_tt * 128: kc * D + (c_or_tt + 1) * 128]
                    rhs = xnT[:, kc * S + sl * 512: kc * S + (sl + 1) * 512]
                else:
                    lhsT = xnT[:, kc * S + c_or_tt * 128: kc * S + (c_or_tt + 1) * 128]
                    rhs = wvs[:, kc * D + sl * 512: kc * D + (sl + 1) * 512]
                st = (kc == 0)
                sp = (kc == KC - 1)
                fn = (lambda lhsT=lhsT, rhs=rhs, st=st, sp=sp, b=b:
                      (lambda t: t.matmul(psQ(b), lhsT, rhs, start=st, stop=sp)))()
                m_g = sch.op("tensor", fn, inc=("ts" if sp else None),
                             waits=(waits if kc == 0 else ()))
            if kind == "q":
                dst = QT[:, c_or_tt * S + sl * 512: c_or_tt * S + (sl + 1) * 512]
            elif kind == "k":
                dst = KT[:, c_or_tt * S + sl * 512: c_or_tt * S + (sl + 1) * 512]
            else:
                dst = Vt[:, c_or_tt * D + sl * 512: c_or_tt * D + (sl + 1) * 512]
            pending_evacs.append((dst, b, m_g))

        def flush_evacs(n):
            k = 0
            while pending_evacs and k < n:
                dst, b, m_g = pending_evacs.pop(0)
                fn = (lambda dst=dst, b=b: (lambda v: v.tensor_copy(dst, psQ(b))))()
                sch.op("vector", fn, inc="esv", waits=[("ts", m_g)])
                k += 1

        for t in range(NT + 2):
            h = t - 2
            if 0 <= h < NT:
                par = h % 2
                stt = st1[par]
                sch.op("scalar", (lambda stt=stt: (lambda s: s.activation(
                    stt[:, 4:5], stt[:, 3:4], AF.Ln, bias=st1[0][:, 15:16])))(),
                    inc="ss", waits=[("vs", m_v1[h]), ("vs", m_eps)])
                m_lnm = sch.cnt["ss"]
                sch.op("scalar", (lambda stt=stt: (lambda s: s.activation(
                    stt[:, 5:6], stt[:, 4:5], AF.Exp, scale=-0.5)))(),
                    inc="ss", waits=[("ss", m_lnm)])
                m_rstd = sch.cnt["ss"]
                sch.op("scalar", (lambda stt=stt: (lambda s: s.activation(
                    stt[:, 6:7], stt[:, 1:2], AF.Identity, scale=stt[:, 5:6])))(),
                    inc="ss", waits=[("ss", m_rstd)])
                m_b2 = sch.cnt["ss"]
                m_xn[h] = sch.op("scalar", (lambda par=par, stt=stt: (lambda s: s.activation(
                    xnbuf[par][:], xbuf[par][:], AF.Identity,
                    scale=stt[:, 5:6], bias=stt[:, 6:7])))(),
                    inc="ss", waits=[("ss", m_b2), ("ts", m_trT.get(h - 2, 0))])
                w = [("ss", m_xn[h]), ("dwi", m_ident)]
                if h >= 2:
                    w.append(("vs", m_evT[h - 2]))
                for c in range(KC):
                    fn = (lambda par=par, c=c: (lambda te: te.transpose(
                        psT(par)[:, c * 128:(c + 1) * 128],
                        xnbuf[par][:, c * 128:(c + 1) * 128], ident[:])))()
                    m_trT[h] = sch.op("tensor", fn,
                                      inc=("ts" if c == KC - 1 else None),
                                      waits=(w if c == 0 else ()))
            if t < NT:
                par = t % 2
                stt = st1[par]
                w = []
                if t >= 2:
                    w = [("ss", m_xn[t - 2]), ("vs", m_red[t - 2])]
                m_dx[t] = sch.op("sync",
                                 dma(xbuf[par][:], x_h[t * 128:(t + 1) * 128, :]),
                                 inc=f"dx{par}", inc_by=16, waits=w)
                m_red[t] = sch.op(
                    "vector", (lambda par=par, stt=stt: (lambda v: v.reduce_sum(
                        stt[:, 0:1], xbuf[par][:], axis=AX.X)))(),
                    inc="vs", waits=[(f"dx{par}", m_dx[t])])
                sch.op("vector", (lambda stt=stt: (lambda v: v.tensor_scalar(
                    stt[:, 1:2], stt[:, 0:1], -1.0 / D, None, ALU.mult)))(),
                    inc="vs", waits=[("vs", m_red[t])])
                m_negm = sch.cnt["vs"]
                sch.op("vector", (lambda stt=stt: (lambda v: v.tensor_mul(
                    stt[:, 7:8], stt[:, 1:2], stt[:, 1:2])))(),
                    inc="vs", waits=[("vs", m_negm)])
                m_mm = sch.cnt["vs"]
                m_sq = sch.op("scalar", (lambda par=par, stt=stt: (lambda s: s.activation(
                    junk[:], xbuf[par][:], AF.Square,
                    accum_out=stt[:, 2:3])))(),
                    inc="ss", waits=[(f"dx{par}", m_dx[t]), ("ss", m_junkw[0])])
                m_junkw[0] = m_sq
                m_v1[t] = sch.op("vector", (lambda stt=stt: (lambda v: v.scalar_tensor_tensor(
                    stt[:, 3:4], stt[:, 2:3], 1.0 / D, stt[:, 7:8],
                    ALU.mult, ALU.subtract)))(),
                    inc="vs", waits=[("ss", m_sq), ("vs", m_mm)])
            if 0 <= h < NT and h % 4 == 0 and h >= 4:
                flush_evacs(10 ** 9)
            if 0 <= h < NT:
                par = h % 2
                m_evT[h] = sch.op("vector", (lambda par=par, h=h: (lambda v: v.tensor_copy(
                    xnT[:].rearrange("p (c s) -> p c s", s=S)[:, :, h * 128:(h + 1) * 128],
                    psBF[:].rearrange("p (q c u) -> p q c u", q=2, u=128)[:, par])))(),
                    inc="vs", waits=[("ts", m_trT[h])])
                if h % 4 == 3:
                    s4 = h // 4
                    for c in range(KC):
                        emit_qkv_group("q", c, s4, m_evT[h])
                    for c in range(KC):
                        emit_qkv_group("k", c, s4, m_evT[h])
                    for tt in range(s4 * 4, s4 * 4 + 4):
                        for s2 in range(2):
                            emit_qkv_group("v", tt, s2, m_evT[tt])
        flush_evacs(10 ** 9)
        assert g_ctr[0] == n_groups_total

        m_ph2_end = sch.cnt["ts"]

        m_wsb = sch.op("gpsimd", dma(wsb[:], wsb_h), inc="dws", inc_by=16,
                       waits=[("ts", m_ph2_end)])
        m_bb = sch.op("gpsimd", dma(bb[:], bb_h), inc="dwb", inc_by=16)

        m_exp = {}
        m_trA = {}
        m_evATa = {}
        m_evATb = {}
        m_av = {}
        m_x1 = {}
        m_x1b = {}
        m_sq2 = {}
        m_xn1 = {}
        m_xqf = {}
        m_rnd = {}
        m_trX = {}
        m_evXQ = {}
        m_ffn = {}
        m_stt2 = {}
        m_out = {}
        m_store = {}
        m_xr = {}
        m_amax = {}
        m_inv = {}
        m_b3m = {}
        m_negm2 = {}
        m_rv2 = {}
        m_b22 = {}
        m_sc = {}
        m_mm2h = {}
        m_linvm = {}

        for it in range(NT + 2):
            i = it
            h = it - 1
            j = it - 2
            par = i % 2
            hpar = h % 2
            jpar = j % 2
            stt = st3[i % 4]
            hst = st3[h % 4]
            jst = st3[j % 4]
            nch = i + 1
            nb = nch * 128

            if i < NT:
                w = [("ts", m_ph2_end)]
                if i >= 2:
                    w.append(("vs", m_x1[i - 2]))
                m_xr[i] = sch.op("sync",
                                 dma(xr[par][:], x_h[i * 128:(i + 1) * 128, :]),
                                 inc=f"dr{par}", inc_by=16, waits=w)

            if i < NT:
                w = [("esv", n_groups_total), ("vs", m_evT[NT - 1])]
                if i >= 1:
                    w.append(("ss", m_exp[i - 1]))
                nbanks = (nb + 511) // 512
                for kc in range(KC):
                    lhsT = QT[:, kc * S + i * 128: kc * S + (i + 1) * 128]
                    for b in range(nbanks):
                        n = min(512, nb - b * 512)
                        rhs = KT[:, kc * S + b * 512: kc * S + b * 512 + n]
                        st = (kc == 0)
                        sp = (kc == KC - 1)
                        last = sp and (b == nbanks - 1)
                        fn = (lambda lhsT=lhsT, rhs=rhs, st=st, sp=sp, b=b, n=n:
                              (lambda te: te.matmul(score_ap(b, n), lhsT, rhs,
                                                    start=st, stop=sp)))()
                        m_sc[i] = sch.op("tensor", fn, inc=("ts" if last else None),
                                         waits=(w if (b == 0 and kc == 0) else ()))

            if 0 <= h < NT:
                m_sq2[h] = sch.op("scalar", (lambda hpar=hpar, hst=hst: (lambda s: s.activation(
                    junk[:], x1[hpar][:], AF.Square,
                    accum_out=hst[:, 6:7])))(),
                    inc="ss", waits=[("vs", m_x1[h]), ("ss", m_junkw[0])])
                m_junkw[0] = m_sq2[h]
                sch.op("vector", (lambda hpar=hpar, hst=hst: (lambda v: v.reduce_sum(
                    hst[:, 4:5], x1[hpar][:], axis=AX.X)))(),
                    inc="vs", waits=[("vs", m_x1[h])])
                m_red2 = sch.cnt["vs"]
                sch.op("vector", (lambda hst=hst: (lambda v: v.tensor_scalar(
                    hst[:, 5:6], hst[:, 4:5], -1.0 / D, None, ALU.mult)))(),
                    inc="vs", waits=[("vs", m_red2)])
                m_negm2[h] = sch.cnt["vs"]
                sch.op("vector", (lambda hst=hst: (lambda v: v.tensor_mul(
                    hst[:, 7:8], hst[:, 5:6], hst[:, 5:6])))(),
                    inc="vs", waits=[("vs", m_negm2[h])])
                m_mm2h[h] = sch.cnt["vs"]
                sch.op("vector", (lambda hst=hst: (lambda v: v.scalar_tensor_tensor(
                    hst[:, 8:9], hst[:, 6:7], 1.0 / D, hst[:, 7:8],
                    ALU.mult, ALU.subtract)))(),
                    inc="vs", waits=[("ss", m_sq2[h]), ("vs", m_mm2h[h])])
                m_rv2[h] = sch.cnt["vs"]

            if i < NT:
                diag = (psF1[:, i * 128:(i + 1) * 128] if i < 8
                        else psF2[:, (i - 8) * 128:(i - 7) * 128])
                w = [("ts", m_sc[i]), ("dwm", m_mask)]
                if i >= 4:
                    w += [("vs", m_stt2.get(i - 4, 0)), ("ss", m_xqf.get(i - 4, 0))]
                sch.op("vector", (lambda diag=diag: (lambda v: v.tensor_add(
                    diag, diag, mask[:])))(), inc="vs", waits=w)
                m_maskd = sch.cnt["vs"]
                n1 = min(nb, 1024)
                sch.op("vector", (lambda n1=n1, stt=stt: (lambda v: v.reduce_max(
                    stt[:, 0:1], psF1[:, 0:n1], axis=AX.X, negate=True)))(),
                    inc="vs", waits=[("vs", m_maskd)])
                if nb > 1024:
                    sch.op("vector", (lambda nb=nb, stt=stt: (lambda v: v.reduce_max(
                        stt[:, 15:16], psF2[:, 0:nb - 1024], axis=AX.X,
                        negate=True)))(), inc="vs")
                    sch.op("vector", (lambda stt=stt: (lambda v: v.tensor_tensor(
                        stt[:, 0:1], stt[:, 0:1], stt[:, 15:16], ALU.min)))(),
                        inc="vs", waits=[("vs", sch.cnt["vs"])])
                m_nmax = sch.cnt["vs"]
                sch.op("vector", (lambda stt=stt: (lambda v: v.tensor_scalar(
                    stt[:, 1:2], stt[:, 0:1], 1.0 / 32.0, None, ALU.mult)))(),
                    inc="vs", waits=[("vs", m_nmax)])
                m_b3m[i] = sch.cnt["vs"]

            if 0 <= h < NT:
                sch.op("scalar", (lambda hst=hst: (lambda s: s.activation(
                    hst[:, 9:10], hst[:, 8:9], AF.Ln, bias=st1[0][:, 15:16])))(),
                    inc="ss", waits=[("vs", m_rv2[h])])
                m_ln2 = sch.cnt["ss"]
                sch.op("scalar", (lambda hst=hst: (lambda s: s.activation(
                    hst[:, 10:11], hst[:, 9:10], AF.Exp, scale=-0.5)))(),
                    inc="ss", waits=[("ss", m_ln2)])
                m_rstd2 = sch.cnt["ss"]
                sch.op("scalar", (lambda hst=hst: (lambda s: s.activation(
                    hst[:, 11:12], hst[:, 5:6], AF.Identity,
                    scale=hst[:, 10:11])))(),
                    inc="ss", waits=[("ss", m_rstd2)])
                m_b22[h] = sch.cnt["ss"]

            if j >= 0:
                w = [("vs", m_rnd[j]),
                     ("ss", m_evATb.get(i - 1, m_evXQ.get(j - 1, 0)))]
                for c in range(KC):
                    fn = (lambda c=c, jpar=jpar: (lambda te: te.transpose(
                        psXQ[:, c * 128:(c + 1) * 128],
                        xq[jpar][:, c * 128:(c + 1) * 128], ident[:])))()
                    m_trX[j] = sch.op("tensor", fn,
                                      inc=("ts" if c == KC - 1 else None),
                                      waits=(w if c == 0 else ()))
                m_evXQ[j] = sch.op("scalar", (lambda jpar=jpar: (lambda s: s.copy(
                    xqT[jpar][:], psXQ[:])))(),
                    inc="ss", waits=[("ts", m_trX[j])])

            if i < NT:
                n1 = min(nb, 1024)
                w = [("vs", m_b3m[i])]
                if i >= 1:
                    w.append(("ts", m_trA[i - 1]))
                m_exp[i] = sch.op("scalar", (lambda n1=n1, stt=stt: (lambda s: s.activation(
                    A_sb[:, 0:n1], psF1[:, 0:n1], AF.Exp, scale=1.0 / 32.0,
                    bias=stt[:, 1:2], accum_out=stt[:, 2:3])))(),
                    inc="ss", waits=w)
                if nb > 1024:
                    m_exp[i] = sch.op("scalar", (lambda nb=nb, stt=stt: (lambda s: s.activation(
                        A_sb[:, 1024:nb], psF2[:, 0:nb - 1024], AF.Exp,
                        scale=1.0 / 32.0, bias=stt[:, 1:2],
                        accum_out=stt[:, 14:15])))(), inc="ss")

            if j >= 0:
                w = [("ss", m_evXQ[j]), ("dwt", m_wqt),
                     ("vs", m_x1.get(j + 1, 0)), ("vs", m_stt2.get(j - 1, 0))]
                first = True
                for kc in range(KC):
                    for sl in range(2):
                        st = (kc == 0)
                        sp = (kc == KC - 1)
                        fn = (lambda kc=kc, sl=sl, st=st, sp=sp, jpar=jpar:
                              (lambda te: te.matmul(
                                  psMM[:, sl * 512:(sl + 1) * 512],
                                  xqT[jpar][:, kc * 128:(kc + 1) * 128],
                                  wqt[:, kc * D + sl * 512: kc * D + (sl + 1) * 512],
                                  start=st, stop=sp)))()
                        m_ffn[j] = sch.op("tensor", fn,
                                          inc=("ts" if (sp and sl == 1) else None),
                                          waits=(w if first else ()))
                        first = False

            if i < NT:
                if nb > 1024:
                    sch.op("vector", (lambda stt=stt: (lambda v: v.tensor_add(
                        stt[:, 2:3], stt[:, 2:3], stt[:, 14:15])))(),
                        inc="vs", waits=[("ss", m_exp[i])])
                    lw = [("vs", sch.cnt["vs"])]
                else:
                    lw = [("ss", m_exp[i])]
                sch.op("vector", (lambda stt=stt: (lambda v: v.reciprocal(
                    stt[:, 3:4], stt[:, 2:3])))(),
                    inc="vs", waits=lw)
                m_linvm[i] = sch.cnt["vs"]

            if j >= 0:
                m_stt2[j] = sch.op("vector", (lambda jst=jst: (lambda v: v.scalar_tensor_tensor(
                    t2[:], psMM[:], jst[:, 13:14], wsb[:], ALU.mult, ALU.mult)))(),
                    inc="vs", waits=[("ts", m_ffn[j]), ("dws", m_wsb),
                                     ("gs", m_out.get(j - 1, 0))])
                m_out[j] = sch.op("gpsimd", (lambda jpar=jpar: (lambda g: g.tensor_add(
                    outf[jpar][:], t2[:], x1b[jpar][:])))(),
                    inc="gs", waits=[("vs", m_stt2[j]), ("gs", m_x1b[j]),
                                     (f"do{jpar}", m_store.get(j - 2, 0))])
                m_store[j] = sch.op("sync",
                                    dma(y_h[j * 128:(j + 1) * 128, :], outf[jpar][:]),
                                    inc=f"do{jpar}", inc_by=16,
                                    waits=[("gs", m_out[j])])

            if 0 <= h < NT:
                m_xn1[h] = sch.op("scalar", (lambda hpar=hpar, hst=hst: (lambda s: s.activation(
                    xn1[:], x1[hpar][:], AF.Identity,
                    scale=hst[:, 10:11], bias=hst[:, 11:12])))(),
                    inc="ss", waits=[("ss", m_b22[h]), ("vs", m_amax.get(h - 1, 0))])
                sch.op("vector", (lambda hst=hst: (lambda v: v.reduce_max(
                    hst[:, 12:13], xn1[:], axis=AX.X, apply_absolute_value=True)))(),
                    inc="vs", waits=[("ss", m_xn1[h])])
                m_amax[h] = sch.cnt["vs"]
                sch.op("vector", (lambda hst=hst: (lambda v: v.tensor_scalar(
                    hst[:, 13:14], hst[:, 12:13], Q_EPS, 1.0 / 127.0,
                    ALU.max, ALU.mult)))(),
                    inc="vs", waits=[("vs", m_amax[h])])
                m_xsc = sch.cnt["vs"]
                sch.op("vector", (lambda hst=hst: (lambda v: v.reciprocal(
                    hst[:, 14:15], hst[:, 13:14])))(),
                    inc="vs", waits=[("vs", m_xsc)])
                m_inv[h] = sch.cnt["vs"]

            if i < NT:
                ha = (nch + 1) // 2
                rounds = [(0, ha, 0), (ha, nch, 1)]
                for (lo, hi, bk) in rounds:
                    if lo >= hi:
                        continue
                    w = [("ss", m_exp[i])]
                    if bk == 0:
                        w.append(("ss", m_evATa.get(i - 1, 0)))
                    else:
                        w.append(("ss", m_evXQ.get(j, m_evATb.get(i - 1, 0))))
                        w.append(("ss", m_evATb.get(i - 1, 0)))
                    for jj in range(lo, hi):
                        fn = (lambda jj=jj, lo=lo, bk=bk: (lambda te: te.transpose(
                            psAT[bk][:, (jj - lo) * 128:(jj - lo + 1) * 128],
                            A_sb[:, jj * 128:(jj + 1) * 128], ident[:])))()
                        m_trA[i] = sch.op("tensor", fn,
                                          inc=("ts" if jj == hi - 1 else None),
                                          waits=(w if jj == lo else ()))
                    nb_r = (hi - lo) * 128
                    mev = sch.op("scalar", (lambda lo=lo, nb_r=nb_r, bk=bk: (lambda s: s.copy(
                        AT_sb[:, lo * 128: lo * 128 + nb_r], psAT[bk][:, 0:nb_r])))(),
                        inc="ss", waits=[("ts", m_trA[i])])
                    if bk == 0:
                        m_evATa[i] = mev
                    else:
                        m_evATb[i] = mev
                if nch == 1:
                    m_evATb[i] = m_evATa[i]

            if 0 <= h < NT:
                m_xqf[h] = sch.op("scalar", (lambda hpar=hpar, hst=hst: (lambda s: s.activation(
                    xqf[hpar][:], xn1[:], AF.Identity, scale=hst[:, 14:15])))(),
                    inc="ss", waits=[("vs", m_inv[h]), ("vs", m_rnd.get(h - 2, 0))])
                m_rnd[h] = sch.op("vector", (lambda hpar=hpar: (lambda v: v.tensor_scalar(
                    xq[hpar][:], xqf[hpar][:], C_RND, C_RND,
                    ALU.add, ALU.subtract)))(),
                    inc="vs", waits=[("ss", m_xqf[h]), ("ts", m_trX.get(h - 2, 0))])

            if i < NT:
                w = [("ss", m_evATb[i]), ("ss", m_evATa[i]),
                     ("vs", m_stt2.get(i - 2, 0))]
                first = True
                for jj in range(nch):
                    for sl in range(2):
                        st = (jj == 0)
                        sp = (jj == nch - 1)
                        fn = (lambda jj=jj, sl=sl, st=st, sp=sp: (lambda te: te.matmul(
                            psMM[:, sl * 512:(sl + 1) * 512],
                            AT_sb[:, jj * 128:(jj + 1) * 128],
                            Vt[:, jj * D + sl * 512: jj * D + (sl + 1) * 512],
                            start=st, stop=sp)))()
                        m_av[i] = sch.op("tensor", fn,
                                         inc=("ts" if (sp and sl == 1) else None),
                                         waits=(w if first else ()))
                        first = False
                m_x1[i] = sch.op("vector", (lambda par=par, stt=stt: (lambda v: v.scalar_tensor_tensor(
                    x1[par][:], psMM[:], stt[:, 3:4], xr[par][:],
                    ALU.mult, ALU.add)))(),
                    inc="vs", waits=[("ts", m_av[i]), (f"dr{par}", m_xr[i]),
                                     ("vs", m_linvm[i]),
                                     ("gs", m_x1b.get(i - 2, 0)),
                                     ("ss", m_xn1.get(i - 2, 0))])
                m_x1b[i] = sch.op("gpsimd", (lambda par=par: (lambda g: g.tensor_add(
                    x1b[par][:], x1[par][:], bb[:])))(),
                    inc="gs", waits=[("vs", m_x1[i]), ("dwb", m_bb),
                                     ("gs", m_out.get(i - 2, 0))])

        sch.wait("sync", "do0", m_store[NT - 2])
        sch.wait("sync", "do1", m_store[NT - 1])

        block = e(nc.Block())

        def replay(name):
            def f(eng):
                for itm in sch.items[name]:
                    if itm[0] == "wait":
                        eng.wait_ge(sems[itm[1]], itm[2])
                    else:
                        _, fn, inc, by = itm
                        instr = fn(eng)
                        if inc:
                            instr.then_inc(sems[inc], by)
            return f

        block.sync(replay("sync"))
        block.gpsimd(replay("gpsimd"))
        block.tensor(replay("tensor"))
        block.vector(replay("vector"))
        block.scalar(replay("scalar"))

    return nc


_NC_CACHE = None


def _get_nc():
    global _NC_CACHE
    if _NC_CACHE is None:
        _NC_CACHE = build_nc()
    return _NC_CACHE


def kernel(L0_attn_x, L0_attn_Wk, L0_attn_Wq, L0_attn_Wv,
           L0_ffn_weight_q, L0_ffn_weight_scale, L0_ffn_bias):
    x = np.ascontiguousarray(np.asarray(L0_attn_x, dtype=np.float32))
    wk = np.ascontiguousarray(np.asarray(L0_attn_Wk, dtype=np.float32))
    wq = np.ascontiguousarray(np.asarray(L0_attn_Wq, dtype=np.float32))
    wv = np.ascontiguousarray(np.asarray(L0_attn_Wv, dtype=np.float32))
    wqt = np.ascontiguousarray(np.asarray(L0_ffn_weight_q).T)
    wsb = np.ascontiguousarray(np.broadcast_to(
        np.asarray(L0_ffn_weight_scale, dtype=np.float32)[None, :], (128, D)))
    bb = np.ascontiguousarray(np.broadcast_to(
        np.asarray(L0_ffn_bias, dtype=np.float32)[None, :], (128, D)))
    r = np.arange(128)
    mask = np.where(r[:, None] >= r[None, :], 0.0, NEG_MASK).astype(np.float32)
    ident = np.eye(128).astype(ml_dtypes.bfloat16)

    nc = _get_nc()
    in_maps = [dict(x=x[b], wq=wq, wk=wk, wv=wv, wqt=wqt,
                    wsb=wsb, bb=bb, mask=mask, ident=ident) for b in range(8)]
    res = run_bass_kernel_spmd(nc, in_maps, core_ids=list(range(8)))
    out = np.stack([np.asarray(res.results[b]["y"]) for b in range(8)], axis=0)
    return out.astype(np.float32)


# version 05
# speedup vs baseline: 1.3090x; 1.0716x over previous
import numpy as np
import ml_dtypes
from contextlib import ExitStack
import concourse.bass as bass
import concourse.mybir as mybir
from concourse.bass_utils import run_bass_kernel_spmd

F32 = mybir.dt.float32
BF16 = mybir.dt.bfloat16
I8 = mybir.dt.int8
AF = mybir.ActivationFunctionType
ALU = mybir.AluOpType
AX = mybir.AxisListType

S, D = 2048, 1024
NT = S // 128
KC = D // 128
C_RND = 12582912.0
NEG_MASK = -1.0e30
LN_EPS = 1e-5
Q_EPS = 1e-6

ENGINES = ("sync", "gpsimd", "tensor", "vector", "scalar")
SEMS = ("dx0", "dx1", "dx2", "dr0", "dr1", "do0", "do1",
        "dwi", "dwq", "dwk", "dwv", "dwt", "dwm", "dws", "dwb",
        "ts", "vs", "ss", "gs", "esv")


class Sch:

    def __init__(self):
        self.items = {e: [] for e in ENGINES}
        self.cnt = {s: 0 for s in SEMS}
        self.guaranteed = {e: {s: 0 for s in SEMS} for e in ENGINES}

    def wait(self, eng, sem, val):
        if val is None or val <= 0:
            return
        if val > self.guaranteed[eng][sem]:
            self.items[eng].append(("wait", sem, val))
            self.guaranteed[eng][sem] = val

    def op(self, eng, fn, inc=None, inc_by=1, waits=()):
        for (s, v) in waits:
            self.wait(eng, s, v)
        self.items[eng].append(("op", fn, inc, inc_by))
        if inc:
            self.cnt[inc] += inc_by
            return self.cnt[inc]
        return None


def build_nc():
    nc = bass.Bass("TRN2", target_bir_lowering=False,
                   dynamic_dma_scratch_size=8192)

    x_h = nc.dram_tensor("x", [S, D], F32, kind="ExternalInput").ap()
    wq_h = nc.dram_tensor("wq", [D, D], F32, kind="ExternalInput").ap()
    wk_h = nc.dram_tensor("wk", [D, D], F32, kind="ExternalInput").ap()
    wv_h = nc.dram_tensor("wv", [D, D], F32, kind="ExternalInput").ap()
    wqt_h = nc.dram_tensor("wqt", [D, D], I8, kind="ExternalInput").ap()
    wsb_h = nc.dram_tensor("wsb", [128, D], F32, kind="ExternalInput").ap()
    bb_h = nc.dram_tensor("bb", [128, D], F32, kind="ExternalInput").ap()
    mask_h = nc.dram_tensor("mask", [128, 128], F32, kind="ExternalInput").ap()
    ident_h = nc.dram_tensor("ident", [128, 128], BF16, kind="ExternalInput").ap()
    y_h = nc.dram_tensor("y", [S, D], F32, kind="ExternalOutput").ap()

    off = [(nc.sbuf_base + 31) & ~31]

    def at(name, shape, dtype):
        nbytes = int(np.prod(shape[1:])) * mybir.dt.size(dtype)
        nbytes = (nbytes + 31) & ~31
        h = nc.alloc_sbuf_tensor_at(name, shape, dtype, offset=off[0])
        off[0] += nbytes
        return h

    junk = at("junk_sb", [128, D], BF16)
    ident = at("ident_sb", [128, 128], BF16)
    mask = at("mask_sb", [128, 128], F32)
    st1 = [at(f"st1_{i}", [128, 16], F32) for i in range(4)]
    st3 = [at(f"st3_{i}", [128, 16], F32) for i in range(4)]
    QT = at("QT", [128, KC * S], BF16)
    KT = at("KT", [128, KC * S], BF16)
    Vt = at("Vt", [128, NT * D], BF16)
    wqt = at("wqt_sb", [128, KC * D], BF16)

    R = off[0]
    wqs = at("wq_sb", [128, KC * D], BF16)
    wks = at("wk_sb", [128, KC * D], BF16)
    wvs = at("wv_sb", [128, KC * D], BF16)
    xnT = at("xnT", [128, KC * S], BF16)
    xbuf = [at(f"xbuf{i}", [128, D], F32) for i in range(3)]
    xnbuf = [at(f"xnbuf{i}", [128, D], BF16) for i in range(2)]
    p12_end = off[0]
    assert p12_end <= nc.sbuf_top, f"p12 SBUF overflow: {p12_end} > {nc.sbuf_top}"
    off[0] = R
    wsb = at("wsb_sb", [128, D], F32)
    bb = at("bb_sb", [128, D], F32)
    A_sb = at("A_sb", [128, S], BF16)
    AT_sb = at("AT_sb", [128, S], BF16)
    xr = [at(f"xr{i}", [128, D], F32) for i in range(2)]
    x1 = [at(f"x1_{i}", [128, D], F32) for i in range(2)]
    x1b = [at(f"x1b{i}", [128, D], F32) for i in range(2)]
    xn1 = at("xn1", [128, D], F32)
    xqf = at("xqf", [128, D], F32)
    xq = [at(f"xq{i}", [128, D], BF16) for i in range(2)]
    xqT = [at(f"xqT{i}", [128, D], BF16) for i in range(2)]
    t2 = at("t2", [128, D], F32)
    outf = [at(f"outf{i}", [128, D], F32) for i in range(2)]
    assert off[0] <= p12_end
    nc.sbuf_base = p12_end

    psBF = nc.alloc_psum_tensor("psBF", [128, 2048], BF16)
    psF1 = nc.alloc_psum_tensor("psF1", [128, 1024], F32)
    psF2 = nc.alloc_psum_tensor("psF2", [128, 1024], F32)
    psF3 = nc.alloc_psum_tensor("psF3", [128, 1024], F32)

    def psT(par):
        return psBF[:, par * 1024:(par + 1) * 1024]

    def psQ(b):
        t_ = (psF1, psF2, psF3)[b // 2]
        return t_[:, (b % 2) * 512:(b % 2 + 1) * 512]

    def score_ap(b, n):
        t_ = psF1 if b < 2 else psF2
        return t_[:, (b % 2) * 512:(b % 2) * 512 + n]

    psAT = [psBF[:, 0:1024], psBF[:, 1024:2048]]
    psXQ = psBF[:, 1024:2048]
    psMM = psF3

    sch = Sch()

    with ExitStack() as ctx:
        e = ctx.enter_context
        sems = {s: e(nc.semaphore(s)) for s in SEMS}

        def dma(out_ap, in_ap):
            return lambda g: g.dma_start(out_ap, in_ap)

        m_eps = sch.op("vector", (lambda v: v.memset(st1[0][:, 15:16], LN_EPS)),
                       inc="vs")
        m_ident = sch.op("gpsimd", dma(ident[:], ident_h), inc="dwi", inc_by=16)
        m_wq = sch.op("gpsimd", dma(
            wqs[:].rearrange("p (c d) -> p c d", d=D),
            wq_h.rearrange("(c p) d -> p c d", p=128)), inc="dwq", inc_by=16)
        m_wk = sch.op("gpsimd", dma(
            wks[:].rearrange("p (c d) -> p c d", d=D),
            wk_h.rearrange("(c p) d -> p c d", p=128)), inc="dwk", inc_by=16)
        m_wv = sch.op("gpsimd", dma(
            wvs[:].rearrange("p (c d) -> p c d", d=D),
            wv_h.rearrange("(c p) d -> p c d", p=128)), inc="dwv", inc_by=16)

        m_dx = {}
        m_junkw = [0]
        m_red = {}
        m_v1 = {}
        m_xn = {}
        m_trT = {}
        m_evT = {}
        NQB = 6
        g_ctr = [0]
        n_groups_total = 96
        pending_evacs = []

        def emit_qkv_group(kind, c_or_tt, sl, tok_ms):
            g = g_ctr[0]
            g_ctr[0] += 1
            b = g % NQB
            waits = [("vs", tok_ms),
                     ({"q": "dwq", "k": "dwk", "v": "dwv"}[kind],
                      {"q": m_wq, "k": m_wk, "v": m_wv}[kind])]
            if g >= NQB:
                waits.append(("esv", g - NQB + 1))
            m_g = None
            for kc in range(KC):
                if kind in ("q", "k"):
                    wt = wqs if kind == "q" else wks
                    lhsT = wt[:, kc * D + c_or_tt * 128: kc * D + (c_or_tt + 1) * 128]
                    rhs = xnT[:, kc * S + sl * 512: kc * S + (sl + 1) * 512]
                else:
                    lhsT = xnT[:, kc * S + c_or_tt * 128: kc * S + (c_or_tt + 1) * 128]
                    rhs = wvs[:, kc * D + sl * 512: kc * D + (sl + 1) * 512]
                st = (kc == 0)
                sp = (kc == KC - 1)
                fn = (lambda lhsT=lhsT, rhs=rhs, st=st, sp=sp, b=b:
                      (lambda t: t.matmul(psQ(b), lhsT, rhs, start=st, stop=sp)))()
                m_g = sch.op("tensor", fn, inc=("ts" if sp else None),
                             waits=(waits if kc == 0 else ()))
            if kind == "q":
                dst = QT[:, c_or_tt * S + sl * 512: c_or_tt * S + (sl + 1) * 512]
            elif kind == "k":
                dst = KT[:, c_or_tt * S + sl * 512: c_or_tt * S + (sl + 1) * 512]
            else:
                dst = Vt[:, c_or_tt * D + sl * 512: c_or_tt * D + (sl + 1) * 512]
            pending_evacs.append((dst, b, m_g))

        def flush_evacs(n):
            k = 0
            while pending_evacs and k < n:
                dst, b, m_g = pending_evacs.pop(0)
                fn = (lambda dst=dst, b=b: (lambda v: v.tensor_copy(dst, psQ(b))))()
                sch.op("vector", fn, inc="esv", waits=[("ts", m_g)])
                k += 1

        for t in range(NT + 2):
            h = t - 2
            if 0 <= h < NT:
                par = h % 2
                stt = st1[h % 4]
                sch.op("scalar", (lambda stt=stt: (lambda s: s.activation(
                    stt[:, 4:5], stt[:, 3:4], AF.Ln, bias=st1[0][:, 15:16])))(),
                    inc="ss", waits=[("vs", m_v1[h]), ("vs", m_eps)])
                m_lnm = sch.cnt["ss"]
                sch.op("scalar", (lambda stt=stt: (lambda s: s.activation(
                    stt[:, 5:6], stt[:, 4:5], AF.Exp, scale=-0.5)))(),
                    inc="ss", waits=[("ss", m_lnm)])
                m_rstd = sch.cnt["ss"]
                sch.op("scalar", (lambda stt=stt: (lambda s: s.activation(
                    stt[:, 6:7], stt[:, 1:2], AF.Identity, scale=stt[:, 5:6])))(),
                    inc="ss", waits=[("ss", m_rstd)])
                m_b2 = sch.cnt["ss"]
                m_xn[h] = sch.op("scalar", (lambda par=par, hx=h % 3, stt=stt: (lambda s: s.activation(
                    xnbuf[par][:], xbuf[hx][:], AF.Identity,
                    scale=stt[:, 5:6], bias=stt[:, 6:7])))(),
                    inc="ss", waits=[("ss", m_b2), ("ts", m_trT.get(h - 2, 0))])
                w = [("ss", m_xn[h]), ("dwi", m_ident)]
                if h >= 2:
                    w.append(("vs", m_evT[h - 2]))
                for c in range(KC):
                    fn = (lambda par=par, c=c: (lambda te: te.transpose(
                        psT(par)[:, c * 128:(c + 1) * 128],
                        xnbuf[par][:, c * 128:(c + 1) * 128], ident[:])))()
                    m_trT[h] = sch.op("tensor", fn,
                                      inc=("ts" if c == KC - 1 else None),
                                      waits=(w if c == 0 else ()))
            if t < NT:
                par = t % 2
                xpar = t % 3
                stt = st1[t % 4]
                w = []
                if t >= 3:
                    w = [("ss", m_xn[t - 3]), ("vs", m_red[t - 3])]
                m_dx[t] = sch.op("sync",
                                 dma(xbuf[xpar][:], x_h[t * 128:(t + 1) * 128, :]),
                                 inc=f"dx{xpar}", inc_by=16, waits=w)
                m_red[t] = sch.op(
                    "vector", (lambda xpar=xpar, stt=stt: (lambda v: v.reduce_sum(
                        stt[:, 0:1], xbuf[xpar][:], axis=AX.X)))(),
                    inc="vs", waits=[(f"dx{xpar}", m_dx[t])])
                sch.op("vector", (lambda stt=stt: (lambda v: v.tensor_scalar(
                    stt[:, 1:2], stt[:, 0:1], -1.0 / D, None, ALU.mult)))(),
                    inc="vs", waits=[("vs", m_red[t])])
                m_negm = sch.cnt["vs"]
                sch.op("vector", (lambda stt=stt: (lambda v: v.tensor_mul(
                    stt[:, 7:8], stt[:, 1:2], stt[:, 1:2])))(),
                    inc="vs", waits=[("vs", m_negm)])
                m_mm = sch.cnt["vs"]
                m_sq = sch.op("scalar", (lambda xpar=xpar, stt=stt: (lambda s: s.activation(
                    junk[:], xbuf[xpar][:], AF.Square,
                    accum_out=stt[:, 2:3])))(),
                    inc="ss", waits=[(f"dx{xpar}", m_dx[t]), ("ss", m_junkw[0])])
                m_junkw[0] = m_sq
                m_v1[t] = sch.op("vector", (lambda stt=stt: (lambda v: v.scalar_tensor_tensor(
                    stt[:, 3:4], stt[:, 2:3], 1.0 / D, stt[:, 7:8],
                    ALU.mult, ALU.subtract)))(),
                    inc="vs", waits=[("ss", m_sq), ("vs", m_mm)])
            if h in (8, 12):
                flush_evacs(10 ** 9)
            if 0 <= h < NT:
                par = h % 2
                m_evT[h] = sch.op("vector", (lambda par=par, h=h: (lambda v: v.tensor_copy(
                    xnT[:].rearrange("p (c s) -> p c s", s=S)[:, :, h * 128:(h + 1) * 128],
                    psBF[:].rearrange("p (q c u) -> p q c u", q=2, u=128)[:, par])))(),
                    inc="vs", waits=[("ts", m_trT[h])])
                batches = {7: [0], 11: [1], 15: [2, 3]}.get(h, [])
                for s4 in batches:
                    for c in range(KC):
                        emit_qkv_group("q", c, s4, m_evT[s4 * 4 + 3])
                    for c in range(KC):
                        emit_qkv_group("k", c, s4, m_evT[s4 * 4 + 3])
                    for tt in range(s4 * 4, s4 * 4 + 4):
                        for s2 in range(2):
                            emit_qkv_group("v", tt, s2, m_evT[tt])
        flush_evacs(10 ** 9)
        assert g_ctr[0] == n_groups_total
        m_wqt = sch.op("gpsimd", dma(
            wqt[:].rearrange("p (c d) -> p c d", d=D),
            wqt_h.rearrange("(c p) d -> p c d", p=128)), inc="dwt", inc_by=16)
        m_mask = sch.op("gpsimd", dma(mask[:], mask_h), inc="dwm", inc_by=16)

        m_ph2_end = sch.cnt["ts"]

        m_wsb = sch.op("gpsimd", dma(wsb[:], wsb_h), inc="dws", inc_by=16,
                       waits=[("ts", m_ph2_end)])
        m_bb = sch.op("gpsimd", dma(bb[:], bb_h), inc="dwb", inc_by=16)

        m_exp = {}
        m_trA = {}
        m_evATa = {}
        m_evATb = {}
        m_av = {}
        m_x1 = {}
        m_x1b = {}
        m_sq2 = {}
        m_xn1 = {}
        m_xqf = {}
        m_rnd = {}
        m_trX = {}
        m_evXQ = {}
        m_ffn = {}
        m_stt2 = {}
        m_out = {}
        m_store = {}
        m_xr = {}
        m_amax = {}
        m_inv = {}
        m_b3m = {}
        m_negm2 = {}
        m_rv2 = {}
        m_b22 = {}
        m_sc = {}
        m_mm2h = {}
        m_linvm = {}

        for it in range(NT + 2):
            i = it
            h = it - 1
            j = it - 2
            par = i % 2
            hpar = h % 2
            jpar = j % 2
            stt = st3[i % 4]
            hst = st3[h % 4]
            jst = st3[j % 4]
            nch = i + 1
            nb = nch * 128

            if i < NT:
                w = [("ts", m_ph2_end)]
                if i >= 2:
                    w.append(("vs", m_x1[i - 2]))
                m_xr[i] = sch.op("sync",
                                 dma(xr[par][:], x_h[i * 128:(i + 1) * 128, :]),
                                 inc=f"dr{par}", inc_by=16, waits=w)

            if i < NT:
                w = [("esv", n_groups_total), ("vs", m_evT[NT - 1])]
                if i >= 1:
                    w.append(("ss", m_exp[i - 1]))
                nbanks = (nb + 511) // 512
                for kc in range(KC):
                    lhsT = QT[:, kc * S + i * 128: kc * S + (i + 1) * 128]
                    for b in range(nbanks):
                        n = min(512, nb - b * 512)
                        rhs = KT[:, kc * S + b * 512: kc * S + b * 512 + n]
                        st = (kc == 0)
                        sp = (kc == KC - 1)
                        last = sp and (b == nbanks - 1)
                        fn = (lambda lhsT=lhsT, rhs=rhs, st=st, sp=sp, b=b, n=n:
                              (lambda te: te.matmul(score_ap(b, n), lhsT, rhs,
                                                    start=st, stop=sp)))()
                        m_sc[i] = sch.op("tensor", fn, inc=("ts" if last else None),
                                         waits=(w if (b == 0 and kc == 0) else ()))

            if 0 <= h < NT:
                m_sq2[h] = sch.op("scalar", (lambda hpar=hpar, hst=hst: (lambda s: s.activation(
                    junk[:], x1[hpar][:], AF.Square,
                    accum_out=hst[:, 6:7])))(),
                    inc="ss", waits=[("vs", m_x1[h]), ("ss", m_junkw[0])])
                m_junkw[0] = m_sq2[h]
                sch.op("vector", (lambda hpar=hpar, hst=hst: (lambda v: v.reduce_sum(
                    hst[:, 4:5], x1[hpar][:], axis=AX.X)))(),
                    inc="vs", waits=[("vs", m_x1[h])])
                m_red2 = sch.cnt["vs"]
                sch.op("vector", (lambda hst=hst: (lambda v: v.tensor_scalar(
                    hst[:, 5:6], hst[:, 4:5], -1.0 / D, None, ALU.mult)))(),
                    inc="vs", waits=[("vs", m_red2)])
                m_negm2[h] = sch.cnt["vs"]
                sch.op("vector", (lambda hst=hst: (lambda v: v.tensor_mul(
                    hst[:, 7:8], hst[:, 5:6], hst[:, 5:6])))(),
                    inc="vs", waits=[("vs", m_negm2[h])])
                m_mm2h[h] = sch.cnt["vs"]
                sch.op("vector", (lambda hst=hst: (lambda v: v.scalar_tensor_tensor(
                    hst[:, 8:9], hst[:, 6:7], 1.0 / D, hst[:, 7:8],
                    ALU.mult, ALU.subtract)))(),
                    inc="vs", waits=[("ss", m_sq2[h]), ("vs", m_mm2h[h])])
                m_rv2[h] = sch.cnt["vs"]

            if i < NT:
                diag = (psF1[:, i * 128:(i + 1) * 128] if i < 8
                        else psF2[:, (i - 8) * 128:(i - 7) * 128])
                w = [("ts", m_sc[i]), ("dwm", m_mask)]
                if i >= 4:
                    w += [("vs", m_stt2.get(i - 4, 0)), ("ss", m_xqf.get(i - 4, 0))]
                sch.op("vector", (lambda diag=diag: (lambda v: v.tensor_add(
                    diag, diag, mask[:])))(), inc="vs", waits=w)
                m_maskd = sch.cnt["vs"]
                n1 = min(nb, 1024)
                sch.op("vector", (lambda n1=n1, stt=stt: (lambda v: v.reduce_max(
                    stt[:, 0:1], psF1[:, 0:n1], axis=AX.X, negate=True)))(),
                    inc="vs", waits=[("vs", m_maskd)])
                if nb > 1024:
                    sch.op("vector", (lambda nb=nb, stt=stt: (lambda v: v.reduce_max(
                        stt[:, 15:16], psF2[:, 0:nb - 1024], axis=AX.X,
                        negate=True)))(), inc="vs")
                    sch.op("vector", (lambda stt=stt: (lambda v: v.tensor_tensor(
                        stt[:, 0:1], stt[:, 0:1], stt[:, 15:16], ALU.min)))(),
                        inc="vs", waits=[("vs", sch.cnt["vs"])])
                m_nmax = sch.cnt["vs"]
                sch.op("vector", (lambda stt=stt: (lambda v: v.tensor_scalar(
                    stt[:, 1:2], stt[:, 0:1], 1.0 / 32.0, None, ALU.mult)))(),
                    inc="vs", waits=[("vs", m_nmax)])
                m_b3m[i] = sch.cnt["vs"]

            if 0 <= h < NT:
                sch.op("scalar", (lambda hst=hst: (lambda s: s.activation(
                    hst[:, 9:10], hst[:, 8:9], AF.Ln, bias=st1[0][:, 15:16])))(),
                    inc="ss", waits=[("vs", m_rv2[h])])
                m_ln2 = sch.cnt["ss"]
                sch.op("scalar", (lambda hst=hst: (lambda s: s.activation(
                    hst[:, 10:11], hst[:, 9:10], AF.Exp, scale=-0.5)))(),
                    inc="ss", waits=[("ss", m_ln2)])
                m_rstd2 = sch.cnt["ss"]
                sch.op("scalar", (lambda hst=hst: (lambda s: s.activation(
                    hst[:, 11:12], hst[:, 5:6], AF.Identity,
                    scale=hst[:, 10:11])))(),
                    inc="ss", waits=[("ss", m_rstd2)])
                m_b22[h] = sch.cnt["ss"]

            if j >= 0:
                w = [("vs", m_rnd[j]),
                     ("ss", m_evATb.get(i - 1, m_evXQ.get(j - 1, 0)))]
                for c in range(KC):
                    fn = (lambda c=c, jpar=jpar: (lambda te: te.transpose(
                        psXQ[:, c * 128:(c + 1) * 128],
                        xq[jpar][:, c * 128:(c + 1) * 128], ident[:])))()
                    m_trX[j] = sch.op("tensor", fn,
                                      inc=("ts" if c == KC - 1 else None),
                                      waits=(w if c == 0 else ()))
                m_evXQ[j] = sch.op("scalar", (lambda jpar=jpar: (lambda s: s.copy(
                    xqT[jpar][:], psXQ[:])))(),
                    inc="ss", waits=[("ts", m_trX[j])])

            if i < NT:
                n1 = min(nb, 1024)
                w = [("vs", m_b3m[i])]
                if i >= 1:
                    w.append(("ts", m_trA[i - 1]))
                m_exp[i] = sch.op("scalar", (lambda n1=n1, stt=stt: (lambda s: s.activation(
                    A_sb[:, 0:n1], psF1[:, 0:n1], AF.Exp, scale=1.0 / 32.0,
                    bias=stt[:, 1:2], accum_out=stt[:, 2:3])))(),
                    inc="ss", waits=w)
                if nb > 1024:
                    m_exp[i] = sch.op("scalar", (lambda nb=nb, stt=stt: (lambda s: s.activation(
                        A_sb[:, 1024:nb], psF2[:, 0:nb - 1024], AF.Exp,
                        scale=1.0 / 32.0, bias=stt[:, 1:2],
                        accum_out=stt[:, 14:15])))(), inc="ss")

            if j >= 0:
                w = [("ss", m_evXQ[j]), ("dwt", m_wqt),
                     ("vs", m_x1.get(j + 1, 0)), ("vs", m_stt2.get(j - 1, 0))]
                first = True
                for kc in range(KC):
                    for sl in range(2):
                        st = (kc == 0)
                        sp = (kc == KC - 1)
                        fn = (lambda kc=kc, sl=sl, st=st, sp=sp, jpar=jpar:
                              (lambda te: te.matmul(
                                  psMM[:, sl * 512:(sl + 1) * 512],
                                  xqT[jpar][:, kc * 128:(kc + 1) * 128],
                                  wqt[:, kc * D + sl * 512: kc * D + (sl + 1) * 512],
                                  start=st, stop=sp)))()
                        m_ffn[j] = sch.op("tensor", fn,
                                          inc=("ts" if (sp and sl == 1) else None),
                                          waits=(w if first else ()))
                        first = False

            if i < NT:
                if nb > 1024:
                    sch.op("vector", (lambda stt=stt: (lambda v: v.tensor_add(
                        stt[:, 2:3], stt[:, 2:3], stt[:, 14:15])))(),
                        inc="vs", waits=[("ss", m_exp[i])])
                    lw = [("vs", sch.cnt["vs"])]
                else:
                    lw = [("ss", m_exp[i])]
                sch.op("vector", (lambda stt=stt: (lambda v: v.reciprocal(
                    stt[:, 3:4], stt[:, 2:3])))(),
                    inc="vs", waits=lw)
                m_linvm[i] = sch.cnt["vs"]

            if j >= 0:
                m_stt2[j] = sch.op("vector", (lambda jst=jst: (lambda v: v.scalar_tensor_tensor(
                    t2[:], psMM[:], jst[:, 13:14], wsb[:], ALU.mult, ALU.mult)))(),
                    inc="vs", waits=[("ts", m_ffn[j]), ("dws", m_wsb),
                                     ("gs", m_out.get(j - 1, 0))])
                m_out[j] = sch.op("gpsimd", (lambda jpar=jpar: (lambda g: g.tensor_add(
                    outf[jpar][:], t2[:], x1b[jpar][:])))(),
                    inc="gs", waits=[("vs", m_stt2[j]), ("gs", m_x1b[j]),
                                     (f"do{jpar}", m_store.get(j - 2, 0))])
                m_store[j] = sch.op("sync",
                                    dma(y_h[j * 128:(j + 1) * 128, :], outf[jpar][:]),
                                    inc=f"do{jpar}", inc_by=16,
                                    waits=[("gs", m_out[j])])

            if 0 <= h < NT:
                m_xn1[h] = sch.op("scalar", (lambda hpar=hpar, hst=hst: (lambda s: s.activation(
                    xn1[:], x1[hpar][:], AF.Identity,
                    scale=hst[:, 10:11], bias=hst[:, 11:12])))(),
                    inc="ss", waits=[("ss", m_b22[h]), ("vs", m_amax.get(h - 1, 0))])
                sch.op("vector", (lambda hst=hst: (lambda v: v.reduce_max(
                    hst[:, 12:13], xn1[:], axis=AX.X, apply_absolute_value=True)))(),
                    inc="vs", waits=[("ss", m_xn1[h])])
                m_amax[h] = sch.cnt["vs"]
                sch.op("vector", (lambda hst=hst: (lambda v: v.tensor_scalar(
                    hst[:, 13:14], hst[:, 12:13], Q_EPS, 1.0 / 127.0,
                    ALU.max, ALU.mult)))(),
                    inc="vs", waits=[("vs", m_amax[h])])
                m_xsc = sch.cnt["vs"]
                sch.op("vector", (lambda hst=hst: (lambda v: v.reciprocal(
                    hst[:, 14:15], hst[:, 13:14])))(),
                    inc="vs", waits=[("vs", m_xsc)])
                m_inv[h] = sch.cnt["vs"]

            if i < NT:
                ha = (nch + 1) // 2
                rounds = [(0, ha, 0), (ha, nch, 1)]
                for (lo, hi, bk) in rounds:
                    if lo >= hi:
                        continue
                    w = [("ss", m_exp[i])]
                    if bk == 0:
                        w.append(("ss", m_evATa.get(i - 1, 0)))
                    else:
                        w.append(("ss", m_evXQ.get(j, m_evATb.get(i - 1, 0))))
                        w.append(("ss", m_evATb.get(i - 1, 0)))
                    for jj in range(lo, hi):
                        fn = (lambda jj=jj, lo=lo, bk=bk: (lambda te: te.transpose(
                            psAT[bk][:, (jj - lo) * 128:(jj - lo + 1) * 128],
                            A_sb[:, jj * 128:(jj + 1) * 128], ident[:])))()
                        m_trA[i] = sch.op("tensor", fn,
                                          inc=("ts" if jj == hi - 1 else None),
                                          waits=(w if jj == lo else ()))
                    nb_r = (hi - lo) * 128
                    mev = sch.op("scalar", (lambda lo=lo, nb_r=nb_r, bk=bk: (lambda s: s.copy(
                        AT_sb[:, lo * 128: lo * 128 + nb_r], psAT[bk][:, 0:nb_r])))(),
                        inc="ss", waits=[("ts", m_trA[i])])
                    if bk == 0:
                        m_evATa[i] = mev
                    else:
                        m_evATb[i] = mev
                if nch == 1:
                    m_evATb[i] = m_evATa[i]

            if 0 <= h < NT:
                m_xqf[h] = sch.op("scalar", (lambda hst=hst: (lambda s: s.activation(
                    xqf[:], xn1[:], AF.Identity, scale=hst[:, 14:15])))(),
                    inc="ss", waits=[("vs", m_inv[h]), ("vs", m_rnd.get(h - 1, 0))])
                m_rnd[h] = sch.op("vector", (lambda hpar=hpar: (lambda v: v.tensor_scalar(
                    xq[hpar][:], xqf[:], C_RND, C_RND,
                    ALU.add, ALU.subtract)))(),
                    inc="vs", waits=[("ss", m_xqf[h]), ("ts", m_trX.get(h - 2, 0))])

            if i < NT:
                ha_av = (nch + 1) // 2
                for jj in range(nch):
                    if jj == 0:
                        w = [("ss", m_evATa[i]), ("vs", m_stt2.get(i - 2, 0))]
                    elif jj == ha_av:
                        w = [("ss", m_evATb[i])]
                    else:
                        w = ()
                    for sl in range(2):
                        st = (jj == 0)
                        sp = (jj == nch - 1)
                        fn = (lambda jj=jj, sl=sl, st=st, sp=sp: (lambda te: te.matmul(
                            psMM[:, sl * 512:(sl + 1) * 512],
                            AT_sb[:, jj * 128:(jj + 1) * 128],
                            Vt[:, jj * D + sl * 512: jj * D + (sl + 1) * 512],
                            start=st, stop=sp)))()
                        m_av[i] = sch.op("tensor", fn,
                                         inc=("ts" if (sp and sl == 1) else None),
                                         waits=(w if sl == 0 else ()))
                m_x1[i] = sch.op("vector", (lambda par=par, stt=stt: (lambda v: v.scalar_tensor_tensor(
                    x1[par][:], psMM[:], stt[:, 3:4], xr[par][:],
                    ALU.mult, ALU.add)))(),
                    inc="vs", waits=[("ts", m_av[i]), (f"dr{par}", m_xr[i]),
                                     ("vs", m_linvm[i]),
                                     ("gs", m_x1b.get(i - 2, 0)),
                                     ("ss", m_xn1.get(i - 2, 0))])
                m_x1b[i] = sch.op("gpsimd", (lambda par=par: (lambda g: g.tensor_add(
                    x1b[par][:], x1[par][:], bb[:])))(),
                    inc="gs", waits=[("vs", m_x1[i]), ("dwb", m_bb),
                                     ("gs", m_out.get(i - 2, 0))])

        sch.wait("sync", "do0", m_store[NT - 2])
        sch.wait("sync", "do1", m_store[NT - 1])

        block = e(nc.Block())

        def replay(name):
            def f(eng):
                for itm in sch.items[name]:
                    if itm[0] == "wait":
                        eng.wait_ge(sems[itm[1]], itm[2])
                    else:
                        _, fn, inc, by = itm
                        instr = fn(eng)
                        if inc:
                            instr.then_inc(sems[inc], by)
            return f

        block.sync(replay("sync"))
        block.gpsimd(replay("gpsimd"))
        block.tensor(replay("tensor"))
        block.vector(replay("vector"))
        block.scalar(replay("scalar"))

    return nc


_NC_CACHE = None


def _get_nc():
    global _NC_CACHE
    if _NC_CACHE is None:
        _NC_CACHE = build_nc()
    return _NC_CACHE


def kernel(L0_attn_x, L0_attn_Wk, L0_attn_Wq, L0_attn_Wv,
           L0_ffn_weight_q, L0_ffn_weight_scale, L0_ffn_bias):
    x = np.ascontiguousarray(np.asarray(L0_attn_x, dtype=np.float32))
    wk = np.ascontiguousarray(np.asarray(L0_attn_Wk, dtype=np.float32))
    wq = np.ascontiguousarray(np.asarray(L0_attn_Wq, dtype=np.float32))
    wv = np.ascontiguousarray(np.asarray(L0_attn_Wv, dtype=np.float32))
    wqt = np.ascontiguousarray(np.asarray(L0_ffn_weight_q).T)
    wsb = np.ascontiguousarray(np.broadcast_to(
        np.asarray(L0_ffn_weight_scale, dtype=np.float32)[None, :], (128, D)))
    bb = np.ascontiguousarray(np.broadcast_to(
        np.asarray(L0_ffn_bias, dtype=np.float32)[None, :], (128, D)))
    r = np.arange(128)
    mask = np.where(r[:, None] >= r[None, :], 0.0, NEG_MASK).astype(np.float32)
    ident = np.eye(128).astype(ml_dtypes.bfloat16)

    nc = _get_nc()
    in_maps = [dict(x=x[b], wq=wq, wk=wk, wv=wv, wqt=wqt,
                    wsb=wsb, bb=bb, mask=mask, ident=ident) for b in range(8)]
    res = run_bass_kernel_spmd(nc, in_maps, core_ids=list(range(8)))
    out = np.stack([np.asarray(res.results[b]["y"]) for b in range(8)], axis=0)
    return out.astype(np.float32)


# version 07
# speedup vs baseline: 1.3407x; 1.0242x over previous
import numpy as np
import ml_dtypes
from contextlib import ExitStack
import concourse.bass as bass
import concourse.mybir as mybir
from concourse.bass_utils import run_bass_kernel_spmd

F32 = mybir.dt.float32
BF16 = mybir.dt.bfloat16
I8 = mybir.dt.int8
AF = mybir.ActivationFunctionType
ALU = mybir.AluOpType
AX = mybir.AxisListType

S, D = 2048, 1024
NT = S // 128
KC = D // 128
C_RND = 12582912.0
NEG_MASK = -1.0e30
LN_EPS = 1e-5
Q_EPS = 1e-6

ENGINES = ("sync", "gpsimd", "tensor", "vector", "scalar")
SEMS = ("dx0", "dx1", "dx2", "dp0", "dp1", "dp2", "dr0", "dr1", "do0", "do1",
        "dwi", "dwq", "dwk", "dwv", "dwt", "dwm", "dws", "dwb",
        "ts", "vs", "ss", "gs", "esv")


class Sch:

    def __init__(self):
        self.items = {e: [] for e in ENGINES}
        self.cnt = {s: 0 for s in SEMS}
        self.guaranteed = {e: {s: 0 for s in SEMS} for e in ENGINES}

    def wait(self, eng, sem, val):
        if val is None or val <= 0:
            return
        if val > self.guaranteed[eng][sem]:
            self.items[eng].append(("wait", sem, val))
            self.guaranteed[eng][sem] = val

    def op(self, eng, fn, inc=None, inc_by=1, waits=()):
        for (s, v) in waits:
            self.wait(eng, s, v)
        self.items[eng].append(("op", fn, inc, inc_by))
        if inc:
            self.cnt[inc] += inc_by
            return self.cnt[inc]
        return None


def build_nc():
    nc = bass.Bass("TRN2", target_bir_lowering=False,
                   dynamic_dma_scratch_size=8192)

    x_h = nc.dram_tensor("x", [S, D], F32, kind="ExternalInput").ap()
    wq_h = nc.dram_tensor("wq", [D, D], F32, kind="ExternalInput").ap()
    wk_h = nc.dram_tensor("wk", [D, D], F32, kind="ExternalInput").ap()
    wv_h = nc.dram_tensor("wv", [D, D], F32, kind="ExternalInput").ap()
    wqt_h = nc.dram_tensor("wqt", [D, D], I8, kind="ExternalInput").ap()
    wsb_h = nc.dram_tensor("wsb", [128, D], F32, kind="ExternalInput").ap()
    bb_h = nc.dram_tensor("bb", [128, D], F32, kind="ExternalInput").ap()
    mask_h = nc.dram_tensor("mask", [128, 128], F32, kind="ExternalInput").ap()
    ident_h = nc.dram_tensor("ident", [128, 128], BF16, kind="ExternalInput").ap()
    y_h = nc.dram_tensor("y", [S, D], F32, kind="ExternalOutput").ap()

    off = [(nc.sbuf_base + 31) & ~31]

    def at(name, shape, dtype):
        nbytes = int(np.prod(shape[1:])) * mybir.dt.size(dtype)
        nbytes = (nbytes + 31) & ~31
        h = nc.alloc_sbuf_tensor_at(name, shape, dtype, offset=off[0])
        off[0] += nbytes
        return h

    junk = at("junk_sb", [128, D], BF16)
    ident = at("ident_sb", [128, 128], BF16)
    mask = at("mask_sb", [128, 128], F32)
    st1 = [at(f"st1_{i}", [128, 16], F32) for i in range(4)]
    st3 = [at(f"st3_{i}", [128, 16], F32) for i in range(4)]
    QT = at("QT", [128, KC * S], BF16)
    KT = at("KT", [128, KC * S], BF16)
    Vt = at("Vt", [128, NT * D], BF16)
    wqt = at("wqt_sb", [128, KC * D], BF16)

    R = off[0]
    wqs = at("wq_sb", [128, KC * D], BF16)
    wks = at("wk_sb", [128, KC * D], BF16)
    wvs = at("wv_sb", [128, KC * D], BF16)
    xnT = at("xnT", [128, KC * S], BF16)
    xbuf = [at(f"xbuf{i}", [128, D], F32) for i in range(3)]
    xnbuf = [at(f"xnbuf{i}", [128, D], BF16) for i in range(2)]
    p12_end = off[0]
    assert p12_end <= nc.sbuf_top, f"p12 SBUF overflow: {p12_end} > {nc.sbuf_top}"
    off[0] = R
    wsb = at("wsb_sb", [128, D], F32)
    bb = at("bb_sb", [128, D], F32)
    A_sb = at("A_sb", [128, S], BF16)
    AT_sb = at("AT_sb", [128, S], BF16)
    xr = [at(f"xr{i}", [128, D], F32) for i in range(2)]
    x1 = [at(f"x1_{i}", [128, D], F32) for i in range(2)]
    x1b = [at(f"x1b{i}", [128, D], F32) for i in range(2)]
    xn1 = at("xn1", [128, D], F32)
    xqf = at("xqf", [128, D], F32)
    xq = [at(f"xq{i}", [128, D], BF16) for i in range(2)]
    xqT = [at(f"xqT{i}", [128, D], BF16) for i in range(2)]
    t2 = at("t2", [128, D], F32)
    outf = [at(f"outf{i}", [128, D], F32) for i in range(2)]
    assert off[0] <= p12_end
    nc.sbuf_base = p12_end

    psBF = nc.alloc_psum_tensor("psBF", [128, 2048], BF16)
    psF1 = nc.alloc_psum_tensor("psF1", [128, 1024], F32)
    psF2 = nc.alloc_psum_tensor("psF2", [128, 1024], F32)
    psF3 = nc.alloc_psum_tensor("psF3", [128, 1024], F32)

    def psT(par):
        return psBF[:, par * 1024:(par + 1) * 1024]

    def psQ(b):
        t_ = (psF1, psF2, psF3)[b // 2]
        return t_[:, (b % 2) * 512:(b % 2 + 1) * 512]

    def score_ap(b, n):
        t_ = psF1 if b < 2 else psF2
        return t_[:, (b % 2) * 512:(b % 2) * 512 + n]

    psAT = [psBF[:, 0:1024], psBF[:, 1024:2048]]
    psXQ = psBF[:, 1024:2048]
    psMM = psF3

    sch = Sch()

    with ExitStack() as ctx:
        e = ctx.enter_context
        sems = {s: e(nc.semaphore(s)) for s in SEMS}

        def dma(out_ap, in_ap):
            return lambda g: g.dma_start(out_ap, in_ap)

        m_eps = sch.op("vector", (lambda v: v.memset(st1[0][:, 15:16], LN_EPS)),
                       inc="vs")
        sch.op("scalar", (lambda s: s.activation(
            st1[0][:, 14:15], st1[0][:, 15:16], AF.Exp)), inc="ss",
            waits=[("vs", m_eps)])
        m_ident = sch.op("gpsimd", dma(ident[:], ident_h), inc="dwi", inc_by=16)
        m_dx = {}
        dxsem = {}
        for t0 in range(3):
            m_dx[t0] = sch.op("gpsimd",
                              dma(xbuf[t0][:], x_h[t0 * 128:(t0 + 1) * 128, :]),
                              inc=f"dp{t0}", inc_by=16)
            dxsem[t0] = f"dp{t0}"
        m_wq = sch.op("gpsimd", dma(
            wqs[:].rearrange("p (c d) -> p c d", d=D),
            wq_h.rearrange("(c p) d -> p c d", p=128)), inc="dwq", inc_by=16)
        m_wk = sch.op("gpsimd", dma(
            wks[:].rearrange("p (c d) -> p c d", d=D),
            wk_h.rearrange("(c p) d -> p c d", p=128)), inc="dwk", inc_by=16)
        m_wv = sch.op("gpsimd", dma(
            wvs[:].rearrange("p (c d) -> p c d", d=D),
            wv_h.rearrange("(c p) d -> p c d", p=128)), inc="dwv", inc_by=16)

        m_junkw = [0]
        m_red = {}
        m_v1 = {}
        m_xn = {}
        m_trT = {}
        m_evT = {}
        NQB = 6
        g_ctr = [0]
        n_groups_total = 96
        pending_evacs = []

        def emit_qkv_group(kind, c_or_tt, sl, tok_ms):
            g = g_ctr[0]
            g_ctr[0] += 1
            b = g % NQB
            waits = [("vs", tok_ms),
                     ({"q": "dwq", "k": "dwk", "v": "dwv"}[kind],
                      {"q": m_wq, "k": m_wk, "v": m_wv}[kind])]
            if g >= NQB:
                waits.append(("esv", g - NQB + 1))
            m_g = None
            for kc in range(KC):
                if kind in ("q", "k"):
                    wt = wqs if kind == "q" else wks
                    lhsT = wt[:, kc * D + c_or_tt * 128: kc * D + (c_or_tt + 1) * 128]
                    rhs = xnT[:, kc * S + sl * 512: kc * S + (sl + 1) * 512]
                else:
                    lhsT = xnT[:, kc * S + c_or_tt * 128: kc * S + (c_or_tt + 1) * 128]
                    rhs = wvs[:, kc * D + sl * 512: kc * D + (sl + 1) * 512]
                st = (kc == 0)
                sp = (kc == KC - 1)
                fn = (lambda lhsT=lhsT, rhs=rhs, st=st, sp=sp, b=b:
                      (lambda t: t.matmul(psQ(b), lhsT, rhs, start=st, stop=sp)))()
                m_g = sch.op("tensor", fn, inc=("ts" if sp else None),
                             waits=(waits if kc == 0 else ()))
            if kind == "q":
                dst = QT[:, c_or_tt * S + sl * 512: c_or_tt * S + (sl + 1) * 512]
            elif kind == "k":
                dst = KT[:, c_or_tt * S + sl * 512: c_or_tt * S + (sl + 1) * 512]
            else:
                dst = Vt[:, c_or_tt * D + sl * 512: c_or_tt * D + (sl + 1) * 512]
            pending_evacs.append((dst, b, m_g))

        def flush_evacs(n):
            k = 0
            while pending_evacs and k < n:
                dst, b, m_g = pending_evacs.pop(0)
                fn = (lambda dst=dst, b=b: (lambda v: v.tensor_copy(dst, psQ(b))))()
                sch.op("vector", fn, inc="esv", waits=[("ts", m_g)])
                k += 1

        for t in range(NT + 2):
            h = t - 2
            if 0 <= h < NT:
                par = h % 2
                stt = st1[h % 4]
                sch.op("scalar", (lambda stt=stt: (lambda s: s.activation(
                    stt[:, 4:5], stt[:, 3:4], AF.Ln, bias=st1[0][:, 15:16])))(),
                    inc="ss", waits=[("vs", m_v1[h]), ("vs", m_eps)])
                m_lnm = sch.cnt["ss"]
                sch.op("scalar", (lambda stt=stt: (lambda s: s.activation(
                    stt[:, 5:6], stt[:, 4:5], AF.Exp, scale=-0.5)))(),
                    inc="ss", waits=[("ss", m_lnm)])
                m_rstd = sch.cnt["ss"]
                sch.op("scalar", (lambda stt=stt: (lambda s: s.activation(
                    stt[:, 6:7], stt[:, 1:2], AF.Identity, scale=stt[:, 5:6])))(),
                    inc="ss", waits=[("ss", m_rstd)])
                m_b2 = sch.cnt["ss"]
                m_xn[h] = sch.op("scalar", (lambda par=par, hx=h % 3, stt=stt: (lambda s: s.activation(
                    xnbuf[par][:], xbuf[hx][:], AF.Identity,
                    scale=stt[:, 5:6], bias=stt[:, 6:7])))(),
                    inc="ss", waits=[("ss", m_b2), ("ts", m_trT.get(h - 2, 0))])
                w = [("ss", m_xn[h]), ("dwi", m_ident)]
                if h >= 2:
                    w.append(("vs", m_evT[h - 2]))
                for c in range(KC):
                    fn = (lambda par=par, c=c: (lambda te: te.transpose(
                        psT(par)[:, c * 128:(c + 1) * 128],
                        xnbuf[par][:, c * 128:(c + 1) * 128], ident[:])))()
                    m_trT[h] = sch.op("tensor", fn,
                                      inc=("ts" if c == KC - 1 else None),
                                      waits=(w if c == 0 else ()))
            if t < NT:
                par = t % 2
                xpar = t % 3
                stt = st1[t % 4]
                if t >= 3:
                    w = [("ss", m_xn[t - 3]), ("vs", m_red[t - 3])]
                    m_dx[t] = sch.op("sync",
                                     dma(xbuf[xpar][:],
                                         x_h[t * 128:(t + 1) * 128, :]),
                                     inc=f"dx{xpar}", inc_by=16, waits=w)
                    dxsem[t] = f"dx{xpar}"
                m_red[t] = sch.op(
                    "vector", (lambda xpar=xpar, stt=stt: (lambda v: v.reduce_sum(
                        stt[:, 0:1], xbuf[xpar][:], axis=AX.X)))(),
                    inc="vs", waits=[(dxsem[t], m_dx[t])])
                sch.op("vector", (lambda stt=stt: (lambda v: v.tensor_scalar(
                    stt[:, 1:2], stt[:, 0:1], -1.0 / D, None, ALU.mult)))(),
                    inc="vs", waits=[("vs", m_red[t])])
                m_negm = sch.cnt["vs"]
                sch.op("vector", (lambda stt=stt: (lambda v: v.tensor_mul(
                    stt[:, 7:8], stt[:, 1:2], stt[:, 1:2])))(),
                    inc="vs", waits=[("vs", m_negm)])
                m_mm = sch.cnt["vs"]
                m_sq = sch.op("scalar", (lambda xpar=xpar, stt=stt: (lambda s: s.activation(
                    junk[:], xbuf[xpar][:], AF.Square,
                    accum_out=stt[:, 2:3])))(),
                    inc="ss", waits=[(dxsem[t], m_dx[t]), ("ss", m_junkw[0])])
                m_junkw[0] = m_sq
                m_v1[t] = sch.op("vector", (lambda stt=stt: (lambda v: v.scalar_tensor_tensor(
                    stt[:, 3:4], stt[:, 2:3], 1.0 / D, stt[:, 7:8],
                    ALU.mult, ALU.subtract)))(),
                    inc="vs", waits=[("ss", m_sq), ("vs", m_mm)])
            if h in (8, 12):
                flush_evacs(10 ** 9)
            if 0 <= h < NT:
                par = h % 2
                m_evT[h] = sch.op("vector", (lambda par=par, h=h: (lambda v: v.tensor_copy(
                    xnT[:].rearrange("p (c s) -> p c s", s=S)[:, :, h * 128:(h + 1) * 128],
                    psBF[:].rearrange("p (q c u) -> p q c u", q=2, u=128)[:, par])))(),
                    inc="vs", waits=[("ts", m_trT[h])])
                batches = {7: [0], 11: [1], 15: [2, 3]}.get(h, [])
                for s4 in batches:
                    for c in range(KC):
                        emit_qkv_group("q", c, s4, m_evT[s4 * 4 + 3])
                    for c in range(KC):
                        emit_qkv_group("k", c, s4, m_evT[s4 * 4 + 3])
                    for tt in range(s4 * 4, s4 * 4 + 4):
                        for s2 in range(2):
                            emit_qkv_group("v", tt, s2, m_evT[tt])
        flush_evacs(10 ** 9)
        assert g_ctr[0] == n_groups_total
        m_wqt = sch.op("gpsimd", dma(
            wqt[:].rearrange("p (c d) -> p c d", d=D),
            wqt_h.rearrange("(c p) d -> p c d", p=128)), inc="dwt", inc_by=16)
        m_mask = sch.op("gpsimd", dma(mask[:], mask_h), inc="dwm", inc_by=16)

        m_ph2_end = sch.cnt["ts"]

        m_wsb = sch.op("gpsimd", dma(wsb[:], wsb_h), inc="dws", inc_by=16,
                       waits=[("ts", m_ph2_end)])
        m_bb = sch.op("gpsimd", dma(bb[:], bb_h), inc="dwb", inc_by=16)

        m_exp = {}
        m_trA = {}
        m_evATa = {}
        m_evATb = {}
        m_av = {}
        m_x1 = {}
        m_x1b = {}
        m_sq2 = {}
        m_xn1 = {}
        m_xqf = {}
        m_rnd = {}
        m_trX = {}
        m_evXQ = {}
        m_ffn = {}
        m_stt2 = {}
        m_out = {}
        m_store = {}
        m_xr = {}
        m_amax = {}
        m_inv = {}
        m_b3m = {}
        m_negm2 = {}
        m_rv2 = {}
        m_b22 = {}
        m_sc = {}
        m_mm2h = {}
        m_linvm = {}
        m_trXh = {}
        m_evXQa0 = {}

        for it in range(NT + 2):
            i = it
            h = it - 1
            j = it - 2
            par = i % 2
            hpar = h % 2
            jpar = j % 2
            stt = st3[i % 4]
            hst = st3[h % 4]
            jst = st3[j % 4]
            nch = i + 1
            nb = nch * 128

            if i < NT:
                w = [("ts", m_ph2_end)]
                if i >= 2:
                    w.append(("vs", m_x1[i - 2]))
                m_xr[i] = sch.op("sync",
                                 dma(xr[par][:], x_h[i * 128:(i + 1) * 128, :]),
                                 inc=f"dr{par}", inc_by=16, waits=w)

            if i < NT:
                w = [("esv", n_groups_total), ("vs", m_evT[NT - 1])]
                if i >= 1:
                    w.append(("ss", m_exp[i - 1]))
                nbanks = (nb + 511) // 512
                for kc in range(KC):
                    lhsT = QT[:, kc * S + i * 128: kc * S + (i + 1) * 128]
                    for b in range(nbanks):
                        n = min(512, nb - b * 512)
                        rhs = KT[:, kc * S + b * 512: kc * S + b * 512 + n]
                        st = (kc == 0)
                        sp = (kc == KC - 1)
                        last = sp and (b == nbanks - 1)
                        fn = (lambda lhsT=lhsT, rhs=rhs, st=st, sp=sp, b=b, n=n:
                              (lambda te: te.matmul(score_ap(b, n), lhsT, rhs,
                                                    start=st, stop=sp)))()
                        m_sc[i] = sch.op("tensor", fn, inc=("ts" if last else None),
                                         waits=(w if (b == 0 and kc == 0) else ()))

            if 0 <= h < NT:
                m_sq2[h] = sch.op("scalar", (lambda hpar=hpar, hst=hst: (lambda s: s.activation(
                    junk[:], x1[hpar][:], AF.Square,
                    accum_out=hst[:, 6:7])))(),
                    inc="ss", waits=[("vs", m_x1[h]), ("ss", m_junkw[0])])
                m_junkw[0] = m_sq2[h]
                sch.op("vector", (lambda hpar=hpar, hst=hst: (lambda v: v.reduce_sum(
                    hst[:, 4:5], x1[hpar][:], axis=AX.X)))(),
                    inc="vs", waits=[("vs", m_x1[h])])
                m_red2 = sch.cnt["vs"]
                sch.op("vector", (lambda hst=hst: (lambda v: v.tensor_scalar(
                    hst[:, 5:6], hst[:, 4:5], -1.0 / D, None, ALU.mult)))(),
                    inc="vs", waits=[("vs", m_red2)])
                m_negm2[h] = sch.cnt["vs"]
                sch.op("vector", (lambda hst=hst: (lambda v: v.tensor_mul(
                    hst[:, 7:8], hst[:, 5:6], hst[:, 5:6])))(),
                    inc="vs", waits=[("vs", m_negm2[h])])
                m_mm2h[h] = sch.cnt["vs"]
                sch.op("vector", (lambda hst=hst: (lambda v: v.scalar_tensor_tensor(
                    hst[:, 8:9], hst[:, 6:7], 1.0 / D, hst[:, 7:8],
                    ALU.mult, ALU.subtract)))(),
                    inc="vs", waits=[("ss", m_sq2[h]), ("vs", m_mm2h[h])])
                m_rv2[h] = sch.cnt["vs"]

            if i < NT:
                diag = (psF1[:, i * 128:(i + 1) * 128] if i < 8
                        else psF2[:, (i - 8) * 128:(i - 7) * 128])
                w = [("ts", m_sc[i]), ("dwm", m_mask)]
                if i >= 4:
                    w += [("vs", m_stt2.get(i - 4, 0)), ("ss", m_xqf.get(i - 4, 0))]
                sch.op("vector", (lambda diag=diag: (lambda v: v.tensor_add(
                    diag, diag, mask[:])))(), inc="vs", waits=w)
                m_maskd = sch.cnt["vs"]
                n1 = min(nb, 1024)
                sch.op("vector", (lambda n1=n1, stt=stt: (lambda v: v.reduce_max(
                    stt[:, 0:1], psF1[:, 0:n1], axis=AX.X, negate=True)))(),
                    inc="vs", waits=[("vs", m_maskd)])
                if nb > 1024:
                    sch.op("vector", (lambda nb=nb, stt=stt: (lambda v: v.reduce_max(
                        stt[:, 15:16], psF2[:, 0:nb - 1024], axis=AX.X,
                        negate=True)))(), inc="vs")
                    sch.op("vector", (lambda stt=stt: (lambda v: v.tensor_tensor(
                        stt[:, 0:1], stt[:, 0:1], stt[:, 15:16], ALU.min)))(),
                        inc="vs", waits=[("vs", sch.cnt["vs"])])
                m_nmax = sch.cnt["vs"]
                sch.op("vector", (lambda stt=stt: (lambda v: v.tensor_scalar(
                    stt[:, 1:2], stt[:, 0:1], 1.0 / 32.0, None, ALU.mult)))(),
                    inc="vs", waits=[("vs", m_nmax)])
                m_b3m[i] = sch.cnt["vs"]

            if 0 <= h < NT:
                sch.op("scalar", (lambda hst=hst: (lambda s: s.activation(
                    hst[:, 9:10], hst[:, 8:9], AF.Ln, bias=st1[0][:, 15:16])))(),
                    inc="ss", waits=[("vs", m_rv2[h])])
                m_ln2 = sch.cnt["ss"]
                sch.op("scalar", (lambda hst=hst: (lambda s: s.activation(
                    hst[:, 10:11], hst[:, 9:10], AF.Exp, scale=-0.5)))(),
                    inc="ss", waits=[("ss", m_ln2)])
                m_rstd2 = sch.cnt["ss"]
                sch.op("scalar", (lambda hst=hst: (lambda s: s.activation(
                    hst[:, 11:12], hst[:, 5:6], AF.Identity,
                    scale=hst[:, 10:11])))(),
                    inc="ss", waits=[("ss", m_rstd2)])
                m_b22[h] = sch.cnt["ss"]

            if j >= 0:
                for c in range(KC):
                    if c == 0:
                        w = [("vs", m_rnd[j]),
                             ("ss", m_evATa.get(i - 1, m_evXQa0.get(j - 1, 0)))]
                    elif c == 4:
                        w = [("ss", m_evATb.get(i - 1, m_evXQ.get(j - 1, 0)))]
                    else:
                        w = ()
                    dst = (psAT[0][:, c * 128:(c + 1) * 128] if c < 4
                           else psXQ[:, c * 128:(c + 1) * 128])
                    fn = (lambda dst=dst, c=c, jpar=jpar: (lambda te: te.transpose(
                        dst, xq[jpar][:, c * 128:(c + 1) * 128], ident[:])))()
                    mtr = sch.op("tensor", fn,
                                 inc=("ts" if c in (3, KC - 1) else None),
                                 waits=w)
                    if c == 3:
                        m_trXh[j] = mtr
                    m_trX[j] = mtr
                m_evXQa = sch.op("scalar", (lambda jpar=jpar: (lambda s: s.copy(
                    xqT[jpar][:, 0:512], psAT[0][:, 0:512])))(),
                    inc="ss", waits=[("ts", m_trXh[j])])
                m_evXQa0[j] = m_evXQa
                m_evXQ[j] = sch.op("scalar", (lambda jpar=jpar: (lambda s: s.copy(
                    xqT[jpar][:, 512:1024], psXQ[:, 512:1024])))(),
                    inc="ss", waits=[("ts", m_trX[j])])

            if i < NT:
                n1 = min(nb, 1024)
                w = [("vs", m_b3m[i])]
                if i >= 1:
                    w.append(("ts", m_trA[i - 1]))
                m_exp[i] = sch.op("scalar", (lambda n1=n1, stt=stt: (lambda s: s.activation(
                    A_sb[:, 0:n1], psF1[:, 0:n1], AF.Exp, scale=1.0 / 32.0,
                    bias=stt[:, 1:2], accum_out=stt[:, 2:3])))(),
                    inc="ss", waits=w)
                if nb > 1024:
                    m_exp[i] = sch.op("scalar", (lambda nb=nb, stt=stt: (lambda s: s.activation(
                        A_sb[:, 1024:nb], psF2[:, 0:nb - 1024], AF.Exp,
                        scale=1.0 / 32.0, bias=stt[:, 1:2],
                        accum_out=stt[:, 14:15])))(), inc="ss")

            if j >= 0:
                for kc in range(KC):
                    if kc == 0:
                        w = [("ss", m_evXQa), ("dwt", m_wqt),
                             ("vs", m_x1.get(j + 1, 0)),
                             ("vs", m_stt2.get(j - 1, 0))]
                    elif kc == 4:
                        w = [("ss", m_evXQ[j])]
                    else:
                        w = ()
                    for sl in range(2):
                        st = (kc == 0)
                        sp = (kc == KC - 1)
                        fn = (lambda kc=kc, sl=sl, st=st, sp=sp, jpar=jpar:
                              (lambda te: te.matmul(
                                  psMM[:, sl * 512:(sl + 1) * 512],
                                  xqT[jpar][:, kc * 128:(kc + 1) * 128],
                                  wqt[:, kc * D + sl * 512: kc * D + (sl + 1) * 512],
                                  start=st, stop=sp)))()
                        m_ffn[j] = sch.op("tensor", fn,
                                          inc=("ts" if (sp and sl == 1) else None),
                                          waits=(w if sl == 0 else ()))

            if i < NT:
                if nb > 1024:
                    sch.op("vector", (lambda stt=stt: (lambda v: v.tensor_add(
                        stt[:, 2:3], stt[:, 2:3], stt[:, 14:15])))(),
                        inc="vs", waits=[("ss", m_exp[i])])
                    lw = [("vs", sch.cnt["vs"])]
                else:
                    lw = [("ss", m_exp[i])]
                sch.op("vector", (lambda stt=stt: (lambda v: v.reciprocal(
                    stt[:, 3:4], stt[:, 2:3])))(),
                    inc="vs", waits=lw)
                m_linvm[i] = sch.cnt["vs"]

            if j >= 0:
                m_stt2[j] = sch.op("vector", (lambda jst=jst: (lambda v: v.scalar_tensor_tensor(
                    t2[:], psMM[:], jst[:, 13:14], wsb[:], ALU.mult, ALU.mult)))(),
                    inc="vs", waits=[("ts", m_ffn[j]), ("dws", m_wsb),
                                     ("gs", m_out.get(j - 1, 0))])
                m_out[j] = sch.op("gpsimd", (lambda jpar=jpar: (lambda g: g.tensor_add(
                    outf[jpar][:], t2[:], x1b[jpar][:])))(),
                    inc="gs", waits=[("vs", m_stt2[j]), ("gs", m_x1b[j]),
                                     (f"do{jpar}", m_store.get(j - 2, 0))])
                m_store[j] = sch.op("sync",
                                    dma(y_h[j * 128:(j + 1) * 128, :], outf[jpar][:]),
                                    inc=f"do{jpar}", inc_by=16,
                                    waits=[("gs", m_out[j])])

            if 0 <= h < NT:
                m_xn1[h] = sch.op("scalar", (lambda hpar=hpar, hst=hst: (lambda s: s.activation(
                    xn1[:], x1[hpar][:], AF.Identity,
                    scale=hst[:, 10:11], bias=hst[:, 11:12])))(),
                    inc="ss", waits=[("ss", m_b22[h]), ("vs", m_amax.get(h - 1, 0))])
                sch.op("vector", (lambda hst=hst: (lambda v: v.reduce_max(
                    hst[:, 12:13], xn1[:], axis=AX.X, apply_absolute_value=True)))(),
                    inc="vs", waits=[("ss", m_xn1[h])])
                m_amax[h] = sch.cnt["vs"]
                sch.op("vector", (lambda hst=hst: (lambda v: v.tensor_scalar(
                    hst[:, 13:14], hst[:, 12:13], Q_EPS, 1.0 / 127.0,
                    ALU.max, ALU.mult)))(),
                    inc="vs", waits=[("vs", m_amax[h])])
                m_xsc = sch.cnt["vs"]
                sch.op("vector", (lambda hst=hst: (lambda v: v.reciprocal(
                    hst[:, 14:15], hst[:, 13:14])))(),
                    inc="vs", waits=[("vs", m_xsc)])
                m_inv[h] = sch.cnt["vs"]

            if i < NT:
                ha = (nch + 1) // 2
                rounds = [(0, ha, 0), (ha, nch, 1)]
                for (lo, hi, bk) in rounds:
                    if lo >= hi:
                        continue
                    w = [("ss", m_exp[i])]
                    if bk == 0:
                        w.append(("ss", m_evXQa0.get(j, m_evATa.get(i - 1, 0))))
                        w.append(("ss", m_evATa.get(i - 1, 0)))
                    else:
                        w.append(("ss", m_evXQ.get(j, m_evATb.get(i - 1, 0))))
                        w.append(("ss", m_evATb.get(i - 1, 0)))
                    for jj in range(lo, hi):
                        fn = (lambda jj=jj, lo=lo, bk=bk: (lambda te: te.transpose(
                            psAT[bk][:, (jj - lo) * 128:(jj - lo + 1) * 128],
                            A_sb[:, jj * 128:(jj + 1) * 128], ident[:])))()
                        m_trA[i] = sch.op("tensor", fn,
                                          inc=("ts" if jj == hi - 1 else None),
                                          waits=(w if jj == lo else ()))
                    nb_r = (hi - lo) * 128
                    mev = sch.op("scalar", (lambda lo=lo, nb_r=nb_r, bk=bk: (lambda s: s.copy(
                        AT_sb[:, lo * 128: lo * 128 + nb_r], psAT[bk][:, 0:nb_r])))(),
                        inc="ss", waits=[("ts", m_trA[i])])
                    if bk == 0:
                        m_evATa[i] = mev
                    else:
                        m_evATb[i] = mev
                if nch == 1:
                    m_evATb[i] = m_evATa[i]

            if 0 <= h < NT:
                m_xqf[h] = sch.op("scalar", (lambda hst=hst: (lambda s: s.activation(
                    xqf[:], xn1[:], AF.Identity, scale=hst[:, 14:15])))(),
                    inc="ss", waits=[("vs", m_inv[h]), ("vs", m_rnd.get(h - 1, 0))])
                m_rnd[h] = sch.op("vector", (lambda hpar=hpar: (lambda v: v.tensor_scalar(
                    xq[hpar][:], xqf[:], C_RND, C_RND,
                    ALU.add, ALU.subtract)))(),
                    inc="vs", waits=[("ss", m_xqf[h]), ("ts", m_trX.get(h - 2, 0))])

            if i < NT:
                ha_av = (nch + 1) // 2
                for jj in range(nch):
                    if jj == 0:
                        w = [("ss", m_evATa[i]), ("vs", m_stt2.get(i - 2, 0))]
                    elif jj == ha_av:
                        w = [("ss", m_evATb[i])]
                    else:
                        w = ()
                    for sl in range(2):
                        st = (jj == 0)
                        sp = (jj == nch - 1)
                        fn = (lambda jj=jj, sl=sl, st=st, sp=sp: (lambda te: te.matmul(
                            psMM[:, sl * 512:(sl + 1) * 512],
                            AT_sb[:, jj * 128:(jj + 1) * 128],
                            Vt[:, jj * D + sl * 512: jj * D + (sl + 1) * 512],
                            start=st, stop=sp)))()
                        m_av[i] = sch.op("tensor", fn,
                                         inc=("ts" if (sp and sl == 1) else None),
                                         waits=(w if sl == 0 else ()))
                m_x1[i] = sch.op("vector", (lambda par=par, stt=stt: (lambda v: v.scalar_tensor_tensor(
                    x1[par][:], psMM[:], stt[:, 3:4], xr[par][:],
                    ALU.mult, ALU.add)))(),
                    inc="vs", waits=[("ts", m_av[i]), (f"dr{par}", m_xr[i]),
                                     ("vs", m_linvm[i]),
                                     ("gs", m_x1b.get(i - 2, 0)),
                                     ("ss", m_xn1.get(i - 2, 0))])
                m_x1b[i] = sch.op("gpsimd", (lambda par=par: (lambda g: g.tensor_add(
                    x1b[par][:], x1[par][:], bb[:])))(),
                    inc="gs", waits=[("vs", m_x1[i]), ("dwb", m_bb),
                                     ("gs", m_out.get(i - 2, 0))])

        sch.wait("sync", "do0", m_store[NT - 2])
        sch.wait("sync", "do1", m_store[NT - 1])

        block = e(nc.Block())

        def replay(name):
            def f(eng):
                for itm in sch.items[name]:
                    if itm[0] == "wait":
                        eng.wait_ge(sems[itm[1]], itm[2])
                    else:
                        _, fn, inc, by = itm
                        instr = fn(eng)
                        if inc:
                            instr.then_inc(sems[inc], by)
            return f

        block.sync(replay("sync"))
        block.gpsimd(replay("gpsimd"))
        block.tensor(replay("tensor"))
        block.vector(replay("vector"))
        block.scalar(replay("scalar"))

    return nc


_NC_CACHE = None


def _get_nc():
    global _NC_CACHE
    if _NC_CACHE is None:
        _NC_CACHE = build_nc()
    return _NC_CACHE


def kernel(L0_attn_x, L0_attn_Wk, L0_attn_Wq, L0_attn_Wv,
           L0_ffn_weight_q, L0_ffn_weight_scale, L0_ffn_bias):
    x = np.ascontiguousarray(np.asarray(L0_attn_x, dtype=np.float32))
    wk = np.ascontiguousarray(np.asarray(L0_attn_Wk, dtype=np.float32))
    wq = np.ascontiguousarray(np.asarray(L0_attn_Wq, dtype=np.float32))
    wv = np.ascontiguousarray(np.asarray(L0_attn_Wv, dtype=np.float32))
    wqt = np.ascontiguousarray(np.asarray(L0_ffn_weight_q).T)
    wsb = np.ascontiguousarray(np.broadcast_to(
        np.asarray(L0_ffn_weight_scale, dtype=np.float32)[None, :], (128, D)))
    bb = np.ascontiguousarray(np.broadcast_to(
        np.asarray(L0_ffn_bias, dtype=np.float32)[None, :], (128, D)))
    r = np.arange(128)
    mask = np.where(r[:, None] >= r[None, :], 0.0, NEG_MASK).astype(np.float32)
    ident = np.eye(128).astype(ml_dtypes.bfloat16)

    nc = _get_nc()
    in_maps = [dict(x=x[b], wq=wq, wk=wk, wv=wv, wqt=wqt,
                    wsb=wsb, bb=bb, mask=mask, ident=ident) for b in range(8)]
    res = run_bass_kernel_spmd(nc, in_maps, core_ids=list(range(8)))
    out = np.stack([np.asarray(res.results[b]["y"]) for b in range(8)], axis=0)
    return out.astype(np.float32)
